# version 03
# baseline (speedup 1.0000x reference)
import contextlib
import numpy as np
import ml_dtypes
import concourse.bass as bass
import concourse.mybir as mybir
from concourse.bass_utils import run_bass_kernel_spmd
from concourse.replica_groups import maybe_share_collective_output_space

F32 = mybir.dt.float32
BF16 = mybir.dt.bfloat16
I32 = mybir.dt.int32
ALU = mybir.AluOpType
ACT = mybir.ActivationFunctionType
AX = mybir.AxisListType
P = 128
NCORE = 8
M = 4096
NT = 32
D = 1024
MAGIC = 12582912.0
NEG = -1e30


def build(dbg=None):
    nc = bass.Bass("TRN2", target_bir_lowering=False)
    dr = lambda name, shape, dt=F32, kind="ExternalInput", sp=None: nc.dram_tensor(
        name, shape, dt, kind=kind, **({"addr_space": sp} if sp else {})).ap()

    x_in = dr("x_in", [M, D])
    w0_attn = dr("w0_attn", [D, 1536])
    w0_proj = dr("w0_proj", [D, D])
    w0_f1 = dr("w0_f1", [D, 4096])
    b0_f1 = dr("b0_f1", [32, 128])
    w0_f2 = dr("w0_f2", [4096, D])
    b0_f2 = dr("b0_f2", [1, D])
    w1_inp = dr("w1_inp", [D, 3072])
    b1_inp = dr("b1_inp", [1, 3072])
    w1_outp = dr("w1_outp", [D, D])
    b1_outp = dr("b1_outp", [1, D])
    w1_gate = dr("w1_gate", [2816, D])
    w1_up = dr("w1_up", [2816, D])
    w1_down = dr("w1_down", [D, 2816])
    w2_qkv = dr("w2_qkv", [D, 3072])
    b2_qkv = dr("b2_qkv", [1, 3072])
    w2_o = dr("w2_o", [D, D])
    b2_o = dr("b2_o", [1, D])
    w2_pk = dr("w2_pk", [256, D], I32)
    w2_scale = dr("w2_scale", [1, 1])
    b2_f = dr("b2_f", [1, D])
    w3_qkv = dr("w3_qkv", [D, 3072])
    b3_qkv = dr("b3_qkv", [1, 3072])
    w3_l = dr("w3_l", [D, D])
    b3_l = dr("b3_l", [1, D])
    w3_o = dr("w3_o", [D, D])
    b3_o = dr("b3_o", [1, D])
    w3_gate = dr("w3_gate", [3072, D])
    w3_up = dr("w3_up", [3072, D])
    w3_down = dr("w3_down", [D, 3072])
    ident = dr("ident", [P, P])
    costab = dr("costab", [2048, 1280])
    sintab = dr("sintab", [2048, 1280])
    masks = dr("masks", [4 * P, 512])

    out_ext = dr("out", [M, D], kind="ExternalOutput")

    X = dr("X", [M, D], kind="Internal")
    XN = dr("XN", [M, D], kind="Internal")
    XNT = dr("XNT", [D, M], BF16, kind="Internal")
    AOUT = dr("AOUT", [M, D], kind="Internal")
    FOUT = dr("FOUT", [M, D], kind="Internal")
    QL = dr("QL", [M, D], BF16, kind="Internal")
    KL = dr("KL", [M, D], BF16, kind="Internal")
    VL = dr("VL", [M, D], BF16, kind="Internal")
    spa = maybe_share_collective_output_space("AllGather", [list(range(NCORE))])
    KG = dr("KG", [NCORE * M, D], BF16, kind="Internal", sp=spa)
    VG = dr("VG", [NCORE * M, D], BF16, kind="Internal", sp=spa)
    HT = dr("HT", [4096, M], BF16, kind="Internal")
    XQT = dr("XQT", [D, M], BF16, kind="Internal")
    HQT = dr("HQT", [2816, M], BF16, kind="Internal")
    QTH = dr("QTH", [D, 2 * 2048], BF16, kind="Internal")
    KTH = dr("KTH", [D, 2 * 2048], BF16, kind="Internal")
    V65H = dr("V65H", [2 * P, 16 * 1040], BF16, kind="Internal")
    UPH = dr("UPH", [M, 3072], kind="Internal")
    GTH = dr("GTH", [M, 3072], kind="Internal")
    G2H = dr("G2H", [M, 3072], kind="Internal")
    UQH = dr("UQH", [M, 3072], kind="Internal")
    OH = dr("OH", [M, 3072], kind="Internal")
    GD = dr("GD", [P, 1], kind="Internal")
    spr = maybe_share_collective_output_space("AllReduce", [list(range(NCORE))])
    ARI = [dr(f"ari{i}", [1, 1], kind="Internal") for i in range(5)]
    ARO = [dr(f"aro{i}", [1, 1], kind="Internal", sp=spr) for i in range(5)]

    es = contextlib.ExitStack()
    sb = lambda n, sh, dt=F32: es.enter_context(nc.sbuf_tensor(n, sh, dt))
    uid = [0]

    def nm(pfx):
        uid[0] += 1
        return f"{pfx}{uid[0]}"

    with es:
        sem = es.enter_context(nc.semaphore("seq"))
        st8 = {"eng": None, "pend": None, "cnt": 0}

        def run(engname, fn):
            multi = engname in ("vector", "scalar", "sync", "gpsimd")
            if engname != st8["eng"] or multi:
                if st8["pend"] is not None:
                    st8["pend"].then_inc(sem, 1)
                    st8["cnt"] += 1
                    st8["pend"] = None
                if st8["cnt"]:
                    getattr(nc, engname).wait_ge(sem, st8["cnt"])
                st8["eng"] = engname
            inst = fn(getattr(nc, engname))
            if engname == "tensor":
                st8["pend"] = inst
            else:
                inc = 16 if engname == "sync" else 1
                inst.then_inc(sem, inc)
                st8["cnt"] += inc

        V = lambda fn: run("vector", fn)
        S = lambda fn: run("scalar", fn)
        TE = lambda fn: run("tensor", fn)
        DMA = lambda fn: run("sync", fn)
        GP = lambda fn: run("gpsimd", fn)

        idf = sb("idf", [P, P])
        idb = sb("idb", [P, P], BF16)
        onesb = sb("onesb", [1, P], BF16)
        eps5 = sb("eps5", [P, 1])
        eps6 = sb("eps6", [P, 1])
        DMA(lambda e: e.dma_start(idf[:], ident))
        V(lambda e: e.tensor_copy(idb[:], idf[:]))
        V(lambda e: e.memset(onesb[:], 1.0))
        V(lambda e: e.memset(eps5[:], 1e-5))
        V(lambda e: e.memset(eps6[:], 1e-6))

        def tr128(psA, src_bf, dst_bf):
            TE(lambda e: e.matmul(psA, src_bf, idb[:], start=True, stop=True,
                                  skip_group_check=True))
            V(lambda e: e.tensor_copy(dst_bf, psA))

        def load_w_bf(dst_sb, src_dram, kchunks, ncols, stg):
            for kc in range(kchunks):
                nsub = (ncols + 511) // 512
                for sbi in range(nsub):
                    w = min(512, ncols - sbi * 512)
                    DMA(lambda e, kc=kc, sbi=sbi, w=w: e.dma_start(
                        stg[:, 0:w],
                        src_dram[kc * P:(kc + 1) * P, sbi * 512:sbi * 512 + w]))
                    V(lambda e, kc=kc, sbi=sbi, w=w: e.tensor_copy(
                        dst_sb[:, kc * ncols + sbi * 512:kc * ncols + sbi * 512 + w],
                        stg[:, 0:w]))

        def load_bias_bf(dst_sb, src_dram, ncols, stg):
            nsub = (ncols + 511) // 512
            for sbi in range(nsub):
                w = min(512, ncols - sbi * 512)
                DMA(lambda e, sbi=sbi, w=w: e.dma_start(
                    stg[0:1, 0:w], src_dram[0:1, sbi * 512:sbi * 512 + w]))
                V(lambda e, sbi=sbi, w=w: e.tensor_copy(
                    dst_sb[0:1, sbi * 512:sbi * 512 + w], stg[0:1, 0:w]))

        def dense(ps, xT_sb, W_sb, N, bias_sb=None, kchunks=8):
            nb = (N + 511) // 512
            for b in range(nb):
                w = min(512, N - b * 512)
                for kc in range(kchunks):
                    TE(lambda e, b=b, w=w, kc=kc: e.matmul(
                        ps[:, b * 512:b * 512 + w],
                        xT_sb[:, kc * P:(kc + 1) * P],
                        W_sb[:, kc * N + b * 512:kc * N + b * 512 + w],
                        start=(kc == 0), stop=(kc == kchunks - 1 and bias_sb is None),
                        skip_group_check=True))
                if bias_sb is not None:
                    TE(lambda e, b=b, w=w: e.matmul(
                        ps[:, b * 512:b * 512 + w], onesb[:],
                        bias_sb[0:1, b * 512:b * 512 + w],
                        start=False, stop=True, skip_group_check=True))

        def load_xT(dst_sb, srcT, col0, kchunks=8):
            for kc in range(kchunks):
                DMA(lambda e, kc=kc: e.dma_start(
                    dst_sb[:, kc * P:(kc + 1) * P],
                    srcT[kc * P:(kc + 1) * P, col0:col0 + P]))

        def ln_stage(src_X=X):
            with contextlib.ExitStack() as ls:
                lb = lambda n, sh, dt=F32: ls.enter_context(nc.sbuf_tensor(nm(n), sh, dt))
                xt = lb("xt", [P, D]); sq = lb("sq", [P, D])
                xnb = lb("xnb", [P, D], BF16); trb = lb("trb", [P, P], BF16)
                mB = lb("m", [P, 1]); ss = lb("ss", [P, 1]); vr = lb("vr", [P, 1])
                iv = lb("iv", [P, 1])
                pst = ls.enter_context(nc.psum_tensor(nm("pst"), [P, P], F32))
                for t in range(NT):
                    DMA(lambda e, t=t: e.dma_start(xt[:], src_X[t * P:(t + 1) * P, :]))
                    V(lambda e: e.tensor_reduce(mB[:], xt[:], AX.X, ALU.add))
                    V(lambda e: e.tensor_scalar(mB[:], mB[:], 1.0 / D, None, ALU.mult))
                    V(lambda e: e.tensor_tensor(sq[:], xt[:], xt[:], ALU.mult))
                    V(lambda e: e.tensor_reduce(ss[:], sq[:], AX.X, ALU.add))
                    V(lambda e: e.tensor_scalar(ss[:], ss[:], 1.0 / D, None, ALU.mult))
                    V(lambda e: e.tensor_tensor(vr[:], mB[:], mB[:], ALU.mult))
                    V(lambda e: e.tensor_tensor(vr[:], ss[:], vr[:], ALU.subtract))
                    S(lambda e: e.activation(vr[:], vr[:], ACT.Sqrt, bias=eps5[:],
                                             scale=1.0))
                    V(lambda e: e.reciprocal(iv[:], vr[:]))
                    V(lambda e: e.tensor_scalar(xt[:], xt[:], mB[:], iv[:],
                                                ALU.subtract, ALU.mult))
                    DMA(lambda e, t=t: e.dma_start(XN[t * P:(t + 1) * P, :], xt[:]))
                    V(lambda e: e.tensor_copy(xnb[:], xt[:]))
                    for c in range(8):
                        tr128(pst[:], xnb[:, c * P:(c + 1) * P], trb[:])
                        DMA(lambda e, t=t, c=c: e.dma_start(
                            XNT[c * P:(c + 1) * P, t * P:(t + 1) * P], trb[:]))

        def attn_core(r, QT_sb, KT_of, V65_sb, v65row, Wo_sb, bo_sb, causal, mlist):
            with contextlib.ExitStack() as ls:
                lb = lambda n, sh, dt=F32: ls.enter_context(nc.sbuf_tensor(nm(n), sh, dt))
                E = lb("E", [P, 512], BF16)
                mtmp = lb("mtmp", [P, 512])
                yf = lb("yf", [P, 65]); rc = lb("rc", [P, 1])
                CT = lb("CT", [P, 4 * D])
                ctb = lb("ctb", [P, D], BF16)
                ctT = lb("ctT", [P, 8 * P], BF16)
                pj_f = lb("pjf", [P, D])
                psS = ls.enter_context(nc.psum_tensor(nm("psS"), [P, 512], F32))
                psY = [ls.enter_context(nc.psum_tensor(nm("psY"), [P, 65], F32))
                       for _ in range(4)]
                psT = ls.enter_context(nc.psum_tensor(nm("psT"), [P, P], F32))
                psJ = ls.enter_context(nc.psum_tensor(nm("psJ"), [P, D], F32))
                for qb in range(4):
                    nsc = qb * 4 + 4 if causal else 16
                    for h in range(16):
                        off = (h % 2) * 64
                        qc = h // 2
                        kbuf, kbase = KT_of(h)
                        kvi = h if v65row == 1040 else h // 4
                        for sc in range(nsc):
                            TE(lambda e, sc=sc, qb=qb, off=off, qc=qc,
                               kbuf=kbuf, kbase=kbase: e.matmul(
                                   psS[:, 0:512],
                                   kbuf[off:off + 64,
                                        kbase + sc * P:kbase + (sc + 1) * P],
                                   QT_sb[off:off + 64,
                                         qc * 2048 + qb * 512:qc * 2048 + qb * 512 + 512],
                                   start=True, stop=True, tile_position=(off, 0),
                                   skip_group_check=True))
                            j = sc - qb * 4
                            if causal and j >= 0:
                                V(lambda e, j=j: e.tensor_tensor(
                                    mtmp[:], psS[:, 0:512], mlist[j], ALU.add))
                                S(lambda e: e.activation(E[:], mtmp[:], ACT.Exp,
                                                         scale=0.125))
                            else:
                                S(lambda e: e.activation(E[:], psS[:, 0:512], ACT.Exp,
                                                         scale=0.125))
                            for qs in range(4):
                                TE(lambda e, qs=qs, sc=sc, kvi=kvi, nsc=nsc: e.matmul(
                                    psY[qs][:, 0:65],
                                    E[:, qs * P:(qs + 1) * P],
                                    V65_sb[:, sc * v65row + kvi * 65:
                                           sc * v65row + kvi * 65 + 65],
                                    start=(sc == 0), stop=(sc == nsc - 1),
                                    skip_group_check=True))
                        for qs in range(4):
                            V(lambda e, qs=qs: e.tensor_copy(yf[:], psY[qs][:, 0:65]))
                            V(lambda e: e.reciprocal(rc[:], yf[:, 64:65]))
                            V(lambda e, qs=qs, h=h: e.tensor_scalar(
                                CT[:, qs * D + h * 64:qs * D + h * 64 + 64],
                                yf[:, 0:64], rc[:], None, ALU.mult))
                    for qs in range(4):
                        V(lambda e, qs=qs: e.tensor_copy(ctb[:], CT[:, qs * D:(qs + 1) * D]))
                        for c in range(8):
                            tr128(psT[:], ctb[:, c * P:(c + 1) * P],
                                  ctT[:, c * P:(c + 1) * P])
                        dense(psJ, ctT, Wo_sb, D, bias_sb=bo_sb)
                        V(lambda e: e.tensor_copy(pj_f[:], psJ[:, 0:D]))
                        t_g = r * 16 + qb * 4 + qs
                        DMA(lambda e, t_g=t_g: e.dma_start(
                            AOUT[t_g * P:(t_g + 1) * P, :], pj_f[:]))

        def rope(qk, rot, cos_t, sin_t, c0, W):
            V(lambda e: e.tensor_scalar(
                rot[:, 0:W].rearrange("p (a two) -> p a two", two=2)[:, :, 0],
                qk[:, c0:c0 + W].rearrange("p (a two) -> p a two", two=2)[:, :, 1],
                -1.0, None, ALU.mult))
            V(lambda e: e.tensor_copy(
                rot[:, 0:W].rearrange("p (a two) -> p a two", two=2)[:, :, 1],
                qk[:, c0:c0 + W].rearrange("p (a two) -> p a two", two=2)[:, :, 0]))
            V(lambda e: e.tensor_tensor(qk[:, c0:c0 + W], qk[:, c0:c0 + W],
                                        cos_t[:, 0:W], ALU.mult))
            V(lambda e: e.tensor_tensor(rot[:, 0:W], rot[:, 0:W],
                                        sin_t[:, 0:W], ALU.mult))
            V(lambda e: e.tensor_tensor(qk[:, c0:c0 + W], qk[:, c0:c0 + W],
                                        rot[:, 0:W], ALU.add))

        def l0_attn():
            with contextlib.ExitStack() as ls:
                lb = lambda n, sh, dt=F32: ls.enter_context(nc.sbuf_tensor(nm(n), sh, dt))
                W_sb = lb("w0a", [P, 8 * 1536], BF16)
                Wo_sb = lb("w0p", [P, 8 * D], BF16)
                stg = lb("stg", [P, 512])
                load_w_bf(W_sb, w0_attn, 8, 1536, stg)
                load_w_bf(Wo_sb, w0_proj, 8, D, stg)
                msk0 = lb("msk", [P, 4 * 512])
                DMA(lambda e: e.dma_start(
                    msk0[:].rearrange("p (j w) -> p j w", j=4),
                    masks.rearrange("(j p) w -> p j w", j=4)))
                mlist = [msk0[:, j * 512:(j + 1) * 512] for j in range(4)]
                for r in range(2):
                    with contextlib.ExitStack() as rs:
                        rb = lambda n, sh, dt=F32: rs.enter_context(
                            nc.sbuf_tensor(nm(n), sh, dt))
                        QT = rb("QT", [P, 8 * 2048], BF16)
                        KD = [rb("KD", [P, 2048], BF16) for _ in range(4)]
                        V65 = rb("V65", [P, 16 * 260], BF16)
                        V(lambda e: e.memset(V65[:], 1.0))
                        with contextlib.ExitStack() as qs_st:
                            qb_ = lambda n, sh, dt=F32: qs_st.enter_context(
                                nc.sbuf_tensor(nm(n), sh, dt))
                            xT = qb_("xT", [P, 8 * P], BF16)
                            qk = qb_("qk", [P, 1536])
                            rot = qb_("rot", [P, 1280])
                            cos_t = qb_("cos", [P, 1280]); sin_t = qb_("sin", [P, 1280])
                            gs = qb_("gs", [P, 20]); sq2 = qb_("sq2", [P, 1280])
                            qkb = qb_("qkb", [P, 1536], BF16)
                            trb = qb_("trb3", [P, P], BF16)
                            psQ = qs_st.enter_context(
                                nc.psum_tensor(nm("psQ"), [P, 1536], F32))
                            psT2 = qs_st.enter_context(
                                nc.psum_tensor(nm("psT2"), [P, P], F32))
                            for stt in range(16):
                                load_xT(xT, XNT, r * 2048 + stt * P)
                                dense(psQ, xT, W_sb, 1536)
                                V(lambda e: e.tensor_copy(qk[:], psQ[:, 0:1536]))
                                DMA(lambda e, stt=stt: e.dma_start(
                                    cos_t[:], costab[stt * P:(stt + 1) * P, :]))
                                DMA(lambda e, stt=stt: e.dma_start(
                                    sin_t[:], sintab[stt * P:(stt + 1) * P, :]))
                                rope(qk, rot, cos_t, sin_t, 0, 1280)
                                V(lambda e: e.tensor_tensor(sq2[:], qk[:, 0:1280],
                                                            qk[:, 0:1280], ALU.mult))
                                V(lambda e: e.tensor_reduce(
                                    gs[:], sq2[:].rearrange("p (g d) -> p g d", d=64),
                                    AX.X, ALU.add))
                                S(lambda e: e.activation(gs[:], gs[:], ACT.Sqrt,
                                                         bias=eps6[:], scale=1.0 / 64))
                                V(lambda e: e.reciprocal(gs[:], gs[:]))
                                V(lambda e: e.tensor_tensor(
                                    qk[:, 0:1280].rearrange("p (g d) -> p g d", d=64),
                                    qk[:, 0:1280].rearrange("p (g d) -> p g d", d=64),
                                    gs[:].rearrange("p (g o) -> p g o", o=1)
                                    .broadcast_to([P, 20, 64]), ALU.mult))
                                V(lambda e: e.tensor_copy(qkb[:, 0:1280], qk[:, 0:1280]))
                                V(lambda e: e.tensor_copy(qkb[:, 1280:1536],
                                                          psQ[:, 1280:1536]))
                                for c in range(8):
                                    tr128(psT2[:], qkb[:, c * P:(c + 1) * P], trb[:])
                                    V(lambda e, c=c, stt=stt: e.tensor_copy(
                                        QT[:, c * 2048 + stt * P:c * 2048 + (stt + 1) * P],
                                        trb[:]))
                                for kv in range(4):
                                    tr128(psT2[0:64, :],
                                          qkb[:, 1024 + kv * 64:1024 + (kv + 1) * 64],
                                          trb[0:64, :])
                                    V(lambda e, kv=kv, stt=stt: e.tensor_copy(
                                        KD[kv][0:64, stt * P:(stt + 1) * P], trb[0:64, :]))
                                    DMA(lambda e, kv=kv, stt=stt: e.dma_start(
                                        KD[kv][64:128, stt * P:(stt + 1) * P],
                                        KD[kv][0:64, stt * P:(stt + 1) * P]))
                                V(lambda e, stt=stt: e.tensor_copy(
                                    V65[:, stt * 260:stt * 260 + 260]
                                    .rearrange("p (k e) -> p k e", e=65)[:, :, 0:64],
                                    qkb[:, 1280:1536].rearrange("p (k d) -> p k d", d=64)))
                        attn_core(r, QT, lambda h: (KD[h // 4], 0), V65, 260,
                                  Wo_sb, None, True, mlist)

        def l23_attn(layer):
            wqkv, bqkv = (w2_qkv, b2_qkv) if layer == 2 else (w3_qkv, b3_qkv)
            wo, bo = (w2_o, b2_o) if layer == 2 else (w3_o, b3_o)
            for r in range(2):
                with contextlib.ExitStack() as ls:
                    lb = lambda n, sh, dt=F32: ls.enter_context(
                        nc.sbuf_tensor(nm(n), sh, dt))
                    W_sb = lb("wA", [P, 8 * 3072], BF16)
                    stg = lb("stg2", [P, 512])
                    bq_sb = lb("bq", [1, 3072], BF16)
                    load_w_bf(W_sb, wqkv, 8, 3072, stg)
                    load_bias_bf(bq_sb, bqkv, 3072, stg)
                    if layer == 3:
                        Wl_sb = lb("wL", [P, 8 * D], BF16)
                        bl_sb = lb("bl", [1, D], BF16)
                        load_w_bf(Wl_sb, w3_l, 8, D, stg)
                        load_bias_bf(bl_sb, b3_l, D, stg)
                    xT = lb("xT2", [P, 8 * P], BF16)
                    qk = lb("qk2", [P, 3072])
                    qkb = lb("qkb2", [P, 3072], BF16)
                    trb = lb("trb4", [P, P], BF16)
                    if layer == 2:
                        rot = lb("rot2", [P, D])
                        cos_t = lb("cos2", [P, D]); sin_t = lb("sin2", [P, D])
                    else:
                        kvT = lb("kvT", [P, 16 * P], BF16)
                    psQ = ls.enter_context(nc.psum_tensor(nm("psQ2"), [P, 3072], F32))
                    psT2 = ls.enter_context(nc.psum_tensor(nm("psT3"), [P, P], F32))
                    for stt in range(16):
                        load_xT(xT, XNT, r * 2048 + stt * P)
                        dense(psQ, xT, W_sb, 3072, bias_sb=bq_sb)
                        V(lambda e: e.tensor_copy(qk[:], psQ[:, 0:3072]))
                        if layer == 2:
                            DMA(lambda e, stt=stt: e.dma_start(
                                cos_t[:], costab[stt * P:(stt + 1) * P, 0:D]))
                            DMA(lambda e, stt=stt: e.dma_start(
                                sin_t[:], sintab[stt * P:(stt + 1) * P, 0:D]))
                            rope(qk, rot, cos_t, sin_t, 0, D)
                            rope(qk, rot, cos_t, sin_t, D, D)
                        V(lambda e: e.tensor_copy(qkb[:], qk[:]))
                        for c in range(8):
                            tr128(psT2[:], qkb[:, c * P:(c + 1) * P], trb[:])
                            DMA(lambda e, c=c, stt=stt: e.dma_start(
                                QTH[c * P:(c + 1) * P, r * 2048 + stt * P:
                                    r * 2048 + (stt + 1) * P], trb[:]))
                        if layer == 2:
                            for c in range(8):
                                tr128(psT2[:], qkb[:, D + c * P:D + (c + 1) * P], trb[:])
                                DMA(lambda e, c=c, stt=stt: e.dma_start(
                                    KTH[c * P:(c + 1) * P, r * 2048 + stt * P:
                                        r * 2048 + (stt + 1) * P], trb[:]))
                            V(lambda e, stt=stt: e.tensor_copy(
                                qkb[:, 0:1040 + 0].rearrange(
                                    "p (k e) -> p k e", e=65)[:, :, 0:64],
                                qk[:, 2 * D:3 * D].rearrange("p (k d) -> p k d", d=64)))
                            V(lambda e: e.memset(
                                qkb[:, 0:1040].rearrange(
                                    "p (k e) -> p k e", e=65)[:, :, 64:65], 1.0))
                            DMA(lambda e, stt=stt, r=r: e.dma_start(
                                V65H[r * P:(r + 1) * P, stt * 1040:(stt + 1) * 1040],
                                qkb[:, 0:1040]))
                        else:
                            for c in range(16):
                                tr128(psT2[:], qkb[:, D + c * P:D + (c + 1) * P],
                                      kvT[:, c * P:(c + 1) * P])
                            dense(psQ, kvT, Wl_sb, D, bias_sb=bl_sb)
                            V(lambda e: e.tensor_copy(qkb[:, 0:D], psQ[:, 0:D]))
                            for c in range(8):
                                tr128(psT2[:], qkb[:, c * P:(c + 1) * P], trb[:])
                                DMA(lambda e, c=c, stt=stt: e.dma_start(
                                    KTH[c * P:(c + 1) * P, r * 2048 + stt * P:
                                        r * 2048 + (stt + 1) * P], trb[:]))
                            dense(psQ, kvT[:, 8 * P:16 * P], Wl_sb, D, bias_sb=bl_sb)
                            V(lambda e: e.tensor_copy(qk[:, 0:D], psQ[:, 0:D]))
                            V(lambda e, stt=stt: e.tensor_copy(
                                qkb[:, 0:1040].rearrange(
                                    "p (k e) -> p k e", e=65)[:, :, 0:64],
                                qk[:, 0:D].rearrange("p (k d) -> p k d", d=64)))
                            V(lambda e: e.memset(
                                qkb[:, 0:1040].rearrange(
                                    "p (k e) -> p k e", e=65)[:, :, 64:65], 1.0))
                            DMA(lambda e, stt=stt, r=r: e.dma_start(
                                V65H[r * P:(r + 1) * P, stt * 1040:(stt + 1) * 1040],
                                qkb[:, 0:1040]))
                with contextlib.ExitStack() as ls:
                    lb = lambda n, sh, dt=F32: ls.enter_context(
                        nc.sbuf_tensor(nm(n), sh, dt))
                    QT = lb("QT2", [P, 8 * 2048], BF16)
                    KT = lb("KT2", [P, 8 * 2048], BF16)
                    V65 = lb("V653", [P, 16 * 1040], BF16)
                    Wo_sb = lb("wO", [P, 8 * D], BF16)
                    stg = lb("stg7", [P, 512])
                    bo_sb = lb("bo2", [1, D], BF16)
                    load_w_bf(Wo_sb, wo, 8, D, stg)
                    load_bias_bf(bo_sb, bo, D, stg)
                    for c in range(8):
                        DMA(lambda e, c=c: e.dma_start(
                            QT[:, c * 2048:(c + 1) * 2048],
                            QTH[c * P:(c + 1) * P, r * 2048:(r + 1) * 2048]))
                        DMA(lambda e, c=c: e.dma_start(
                            KT[:, c * 2048:(c + 1) * 2048],
                            KTH[c * P:(c + 1) * P, r * 2048:(r + 1) * 2048]))
                    DMA(lambda e: e.dma_start(
                        V65[:], V65H[r * P:(r + 1) * P, :]))
                    attn_core(r, QT, lambda h: (KT, (h // 2) * 2048), V65, 1040,
                              Wo_sb, bo_sb, False, None)

        def l1_attn():
            with contextlib.ExitStack() as ls:
                lb = lambda n, sh, dt=F32: ls.enter_context(nc.sbuf_tensor(nm(n), sh, dt))
                W_sb = lb("w1i", [P, 8 * 3072], BF16)
                stg = lb("stg3", [P, 512])
                b_sb = lb("b1i", [1, 3072], BF16)
                load_w_bf(W_sb, w1_inp, 8, 3072, stg)
                load_bias_bf(b_sb, b1_inp, 3072, stg)
                xT = lb("xT3", [P, 8 * P], BF16)
                qvb = lb("qvb", [P, D], BF16)
                psQ = ls.enter_context(nc.psum_tensor(nm("psQ3"), [P, 3072], F32))
                for t in range(NT):
                    load_xT(xT, XNT, t * P)
                    dense(psQ, xT, W_sb, 3072, bias_sb=b_sb)
                    for j, dst in ((0, QL), (1, KL), (2, VL)):
                        V(lambda e, j=j: e.tensor_copy(
                            qvb[:].rearrange("p (h d) -> p h d", d=64),
                            psQ[:, 0:3072].rearrange("p (h t d) -> p h t d",
                                                     t=3, d=64)[:, :, j, :]))
                        DMA(lambda e, t=t, dst=dst: e.dma_start(
                            dst[t * P:(t + 1) * P, :], qvb[:]))
            GP(lambda e: e.collective_compute(
                "AllGather", ALU.bypass, replica_groups=[list(range(NCORE))],
                ins=[KL], outs=[KG]))
            GP(lambda e: e.collective_compute(
                "AllGather", ALU.bypass, replica_groups=[list(range(NCORE))],
                ins=[VL], outs=[VG]))
            with contextlib.ExitStack() as ls:
                lb = lambda n, sh, dt=F32: ls.enter_context(nc.sbuf_tensor(nm(n), sh, dt))
                Kt = [lb("Kt", [P, D]) for _ in range(16)]
                Vt = [lb("Vt", [P, D]) for _ in range(16)]
                kb = lb("kb", [P, D], BF16)
                Wo_sb = lb("w1o", [P, 8 * D], BF16)
                stg = lb("stg4", [P, 512])
                bo_sb = lb("b1o", [1, D], BF16)
                load_w_bf(Wo_sb, w1_outp, 8, D, stg)
                load_bias_bf(bo_sb, b1_outp, D, stg)
                qs_ = [lb("qs", [P, D]) for _ in range(2)]
                SC = lb("SC", [P, 512]); Ee = lb("Ee", [P, 512])
                prod = lb("prod", [P, D])
                Dn = lb("Dn", [P, 32])
                CTx = lb("CTx", [P, D]); tmp2 = lb("tmp2", [P, D])
                ctb2 = lb("ctb2", [P, D], BF16)
                ctT2 = lb("ctT2", [P, 8 * P], BF16)
                pjf = lb("pjf2", [P, D])
                psT4 = ls.enter_context(nc.psum_tensor(nm("psT4"), [P, P], F32))
                psJ2 = ls.enter_context(nc.psum_tensor(nm("psJ2"), [P, D], F32))
                for sc in range(16):
                    for s in range(2):
                        DMA(lambda e, s=s, sc=sc: e.dma_start(
                            kb[:], QL[s * 2048 + sc * P:s * 2048 + (sc + 1) * P, :]))
                        V(lambda e, s=s: e.tensor_copy(qs_[s][:], kb[:]))
                    for tt in range(16):
                        DMA(lambda e, tt=tt, sc=sc: e.dma_start(
                            kb[:], KG[tt * 2048 + sc * P:tt * 2048 + (sc + 1) * P, :]))
                        V(lambda e, tt=tt: e.tensor_copy(Kt[tt][:], kb[:]))
                        DMA(lambda e, tt=tt, sc=sc: e.dma_start(
                            kb[:], VG[tt * 2048 + sc * P:tt * 2048 + (sc + 1) * P, :]))
                        V(lambda e, tt=tt: e.tensor_copy(Vt[tt][:], kb[:]))
                    for s in range(2):
                        for tt in range(16):
                            V(lambda e, s=s, tt=tt: e.tensor_tensor(
                                prod[:], qs_[s][:], Kt[tt][:], ALU.mult))
                            V(lambda e, s=s, tt=tt: e.tensor_reduce(
                                SC[:].rearrange("p (s2 h t) -> p s2 h t", s2=2, h=16)
                                [:, s, :, tt],
                                prod[:].rearrange("p (g d) -> p g d", d=64),
                                AX.X, ALU.add))
                    S(lambda e: e.activation(Ee[:], SC[:], ACT.Exp, scale=0.125))
                    V(lambda e: e.tensor_reduce(
                        Dn[:], Ee[:].rearrange("p (g t) -> p g t", t=16), AX.X, ALU.add))
                    V(lambda e: e.reciprocal(Dn[:], Dn[:]))
                    for s in range(2):
                        for tt in range(16):
                            tgt = CTx if tt == 0 else tmp2
                            V(lambda e, s=s, tt=tt, tgt=tgt: e.tensor_tensor(
                                tgt[:].rearrange("p (h d) -> p h d", d=64),
                                Vt[tt][:].rearrange("p (h d) -> p h d", d=64),
                                Ee[:].rearrange("p (s2 h t) -> p s2 h t", s2=2, h=16)
                                [:, s, :, tt:tt + 1].broadcast_to([P, 16, 64]),
                                ALU.mult))
                            if tt > 0:
                                V(lambda e: e.tensor_tensor(CTx[:], CTx[:], tmp2[:],
                                                            ALU.add))
                        V(lambda e, s=s: e.tensor_tensor(
                            CTx[:].rearrange("p (h d) -> p h d", d=64),
                            CTx[:].rearrange("p (h d) -> p h d", d=64),
                            Dn[:].rearrange("p (s2 h) -> p s2 h", s2=2)[:, s, :]
                            .rearrange("p (h o) -> p h o", o=1)
                            .broadcast_to([P, 16, 64]), ALU.mult))
                        V(lambda e: e.tensor_copy(ctb2[:], CTx[:]))
                        for c in range(8):
                            tr128(psT4[:], ctb2[:, c * P:(c + 1) * P],
                                  ctT2[:, c * P:(c + 1) * P])
                        dense(psJ2, ctT2, Wo_sb, D, bias_sb=bo_sb)
                        V(lambda e: e.tensor_copy(pjf[:], psJ2[:, 0:D]))
                        tg = s * 16 + sc
                        DMA(lambda e, tg=tg: e.dma_start(
                            AOUT[tg * P:(tg + 1) * P, :], pjf[:]))

        def l0_ffn():
            with contextlib.ExitStack() as ls:
                lb = lambda n, sh, dt=F32: ls.enter_context(nc.sbuf_tensor(nm(n), sh, dt))
                W1 = lb("W1", [P, 8 * 4096], BF16)
                stg = lb("stg5", [P, 512])
                b1s = lb("b1s", [P, 32])
                load_w_bf(W1, w0_f1, 8, 4096, stg)
                for wc in range(32):
                    DMA(lambda e, wc=wc: e.dma_start(
                        b1s[:, wc:wc + 1], b0_f1[wc:wc + 1, :].rearrange("o p -> p o"),
                        allow_slow_non_contiguous=True))
                xTn = lb("xTn", [P, 8 * 512], BF16)
                hb = lb("hb", [P, 512], BF16)
                psH = ls.enter_context(nc.psum_tensor(nm("psH"), [P, 512], F32))
                for tb in range(8):
                    for kc in range(8):
                        DMA(lambda e, tb=tb, kc=kc: e.dma_start(
                            xTn[:, kc * 512:(kc + 1) * 512],
                            XNT[kc * P:(kc + 1) * P, tb * 512:(tb + 1) * 512]))
                    for wc in range(32):
                        for kc in range(8):
                            TE(lambda e, wc=wc, kc=kc: e.matmul(
                                psH[:, 0:512],
                                W1[:, kc * 4096 + wc * P:kc * 4096 + (wc + 1) * P],
                                xTn[:, kc * 512:(kc + 1) * 512],
                                start=(kc == 0), stop=(kc == 7),
                                skip_group_check=True))
                        S(lambda e, wc=wc: e.activation(
                            hb[:], psH[:, 0:512], ACT.Relu, bias=b1s[:, wc:wc + 1],
                            scale=1.0))
                        DMA(lambda e, wc=wc, tb=tb: e.dma_start(
                            HT[wc * P:(wc + 1) * P, tb * 512:(tb + 1) * 512], hb[:]))
            with contextlib.ExitStack() as ls:
                lb = lambda n, sh, dt=F32: ls.enter_context(nc.sbuf_tensor(nm(n), sh, dt))
                W2 = lb("W2", [P, 32 * D], BF16)
                stg = lb("stg6", [P, 512])
                load_w_bf(W2, w0_f2, 32, D, stg)
                b2s = lb("b2s", [1, D], BF16)
                load_bias_bf(b2s, b0_f2, D, stg)
                hT = lb("hT", [P, 32 * P], BF16)
                of = lb("of", [P, D])
                psO = ls.enter_context(nc.psum_tensor(nm("psO"), [P, D], F32))
                for t in range(NT):
                    for kc in range(32):
                        DMA(lambda e, t=t, kc=kc: e.dma_start(
                            hT[:, kc * P:(kc + 1) * P],
                            HT[kc * P:(kc + 1) * P, t * P:(t + 1) * P]))
                    dense(psO, hT, W2, D, bias_sb=b2s, kchunks=32)
                    V(lambda e: e.tensor_copy(of[:], psO[:, 0:D]))
                    DMA(lambda e, t=t: e.dma_start(FOUT[t * P:(t + 1) * P, :], of[:]))

        def mk_qbufs(ls, W):
            lb = lambda n, sh, dt=F32: ls.enter_context(nc.sbuf_tensor(nm(n), sh, dt))
            return {"gm": lb("gm", [P, W // 16]), "gi": lb("gi", [P, W // 16]),
                    "tm": lb("tm", [P, W // 2]), "A": lb("A", [P, W]),
                    "B": lb("B", [P, W]), "C": lb("C", [P, W]),
                    "D2": lb("D2", [P, W]), "E2": lb("E2", [P, W])}

        def blockmax_scale(x_ap, W, qdiv, lg, q):
            V(lambda e: e.tensor_tensor(q["E2"][:, 0:W], x_ap, x_ap, ALU.mult))
            cur = W
            for _ in range(lg):
                half = cur // 2
                V(lambda e, cur=cur, half=half: e.tensor_tensor(
                    q["tm"][:, 0:half],
                    q["E2"][:, 0:cur].rearrange("p (g two) -> p g two", two=2)[:, :, 0],
                    q["E2"][:, 0:cur].rearrange("p (g two) -> p g two", two=2)[:, :, 1],
                    ALU.max))
                V(lambda e, half=half: e.tensor_copy(q["E2"][:, 0:half],
                                                     q["tm"][:, 0:half]))
                cur = half
            S(lambda e, cur=cur: e.activation(q["gm"][:, 0:cur], q["E2"][:, 0:cur],
                                              ACT.Sqrt, scale=1.0))
            V(lambda e, cur=cur: e.tensor_scalar(
                q["gm"][:, 0:cur], q["gm"][:, 0:cur], 1.0 / qdiv, 1e-12,
                ALU.mult, ALU.max))
            V(lambda e, cur=cur: e.reciprocal(q["gi"][:, 0:cur], q["gm"][:, 0:cur]))

        def bc(ap, G, d):
            return ap.rearrange("p (g o) -> p g o", o=1).broadcast_to([P, G, d])

        def fp4_qdq(x_ap, out_bf, W, q):
            G = W // 16
            blockmax_scale(x_ap, W, 6.0, 4, q)
            S(lambda e: e.activation(q["A"][:, 0:W], x_ap, ACT.Abs, scale=1.0))
            V(lambda e: e.tensor_tensor(
                q["A"][:, 0:W].rearrange("p (g d) -> p g d", d=16),
                q["A"][:, 0:W].rearrange("p (g d) -> p g d", d=16),
                bc(q["gi"][:, 0:G], G, 16), ALU.mult))
            V(lambda e: e.tensor_scalar(q["B"][:, 0:W], q["A"][:, 0:W], 2.0, MAGIC,
                                        ALU.mult, ALU.add))
            V(lambda e: e.tensor_scalar(q["B"][:, 0:W], q["B"][:, 0:W], -MAGIC, 0.5,
                                        ALU.add, ALU.mult))
            V(lambda e: e.tensor_scalar(q["B"][:, 0:W], q["B"][:, 0:W], 2.0, None,
                                        ALU.min))
            V(lambda e: e.tensor_scalar(q["C"][:, 0:W], q["A"][:, 0:W], MAGIC, -MAGIC,
                                        ALU.add, ALU.add))
            V(lambda e: e.tensor_scalar(q["C"][:, 0:W], q["C"][:, 0:W], 2.0, 4.0,
                                        ALU.max, ALU.min))
            V(lambda e: e.tensor_scalar(q["D2"][:, 0:W], q["A"][:, 0:W], 5.0, 2.0,
                                        ALU.is_gt, ALU.mult))
            V(lambda e: e.tensor_tensor(q["B"][:, 0:W], q["B"][:, 0:W], q["C"][:, 0:W],
                                        ALU.add))
            V(lambda e: e.tensor_scalar(q["D2"][:, 0:W], q["D2"][:, 0:W], -2.0, None,
                                        ALU.add))
            V(lambda e: e.tensor_tensor(q["B"][:, 0:W], q["B"][:, 0:W], q["D2"][:, 0:W],
                                        ALU.add))
            S(lambda e: e.activation(q["E2"][:, 0:W], x_ap, ACT.Sign, scale=1.0))
            V(lambda e: e.tensor_tensor(
                q["E2"][:, 0:W].rearrange("p (g d) -> p g d", d=16),
                q["E2"][:, 0:W].rearrange("p (g d) -> p g d", d=16),
                bc(q["gm"][:, 0:G], G, 16), ALU.mult))
            V(lambda e: e.tensor_tensor(q["B"][:, 0:W], q["B"][:, 0:W], q["E2"][:, 0:W],
                                        ALU.mult))
            V(lambda e: e.tensor_copy(out_bf[:, 0:W], q["B"][:, 0:W]))

        def lpbq_qdq(x_ap, out_bf, W, q):
            G = W // 32
            blockmax_scale(x_ap, W, 7.0, 5, q)
            V(lambda e: e.tensor_tensor(
                q["A"][:, 0:W].rearrange("p (g d) -> p g d", d=32),
                x_ap.rearrange("p (g d) -> p g d", d=32),
                bc(q["gi"][:, 0:G], G, 32), ALU.mult))
            V(lambda e: e.tensor_scalar(q["A"][:, 0:W], q["A"][:, 0:W], MAGIC, -MAGIC,
                                        ALU.add, ALU.add))
            V(lambda e: e.tensor_tensor(
                q["A"][:, 0:W].rearrange("p (g d) -> p g d", d=32),
                q["A"][:, 0:W].rearrange("p (g d) -> p g d", d=32),
                bc(q["gm"][:, 0:G], G, 32), ALU.mult))
            V(lambda e: e.tensor_copy(out_bf[:, 0:W], q["A"][:, 0:W]))

        def l1_ffn():
            with contextlib.ExitStack() as ls:
                lb = lambda n, sh, dt=F32: ls.enter_context(nc.sbuf_tensor(nm(n), sh, dt))
                q = mk_qbufs(ls, D)
                xt = lb("xt4", [P, D]); xqb = lb("xqb", [P, D], BF16)
                trb = lb("trb6", [P, P], BF16)
                psT5 = ls.enter_context(nc.psum_tensor(nm("psT5"), [P, P], F32))
                for t in range(NT):
                    DMA(lambda e, t=t: e.dma_start(xt[:], XN[t * P:(t + 1) * P, :]))
                    fp4_qdq(xt[:], xqb, D, q)
                    for c in range(8):
                        tr128(psT5[:], xqb[:, c * P:(c + 1) * P], trb[:])
                        DMA(lambda e, t=t, c=c: e.dma_start(
                            XQT[c * P:(c + 1) * P, t * P:(t + 1) * P], trb[:]))
            with contextlib.ExitStack() as ls:
                lb = lambda n, sh, dt=F32: ls.enter_context(nc.sbuf_tensor(nm(n), sh, dt))
                WGT = lb("WGT", [P, 8 * 2816], BF16)
                WUT = lb("WUT", [P, 8 * 2816], BF16)
                q = mk_qbufs(ls, 2816)
                wstg = lb("wstg", [P, D]); wqb = lb("wqb", [P, 2816], BF16)
                trb = lb("trb7", [P, P], BF16)
                psT6 = ls.enter_context(nc.psum_tensor(nm("psT6"), [P, P], F32))
                for wsrc, Wdst in ((w1_gate, WGT), (w1_up, WUT)):
                    for oc in range(22):
                        DMA(lambda e, oc=oc, wsrc=wsrc: e.dma_start(
                            wstg[:], wsrc[oc * P:(oc + 1) * P, :]))
                        fp4_qdq(wstg[:], wqb, D, q)
                        for ic in range(8):
                            tr128(psT6[:], wqb[:, ic * P:(ic + 1) * P], trb[:])
                            V(lambda e, ic=ic, oc=oc, Wdst=Wdst: e.tensor_copy(
                                Wdst[:, ic * 2816 + oc * P:ic * 2816 + (oc + 1) * P],
                                trb[:]))
                xT = lb("xT5", [P, 8 * P], BF16)
                Hf = lb("Hf", [P, 2816])
                gsb = lb("gsb", [P, 512])
                hqb = lb("hqb", [P, 2816], BF16)
                psG = ls.enter_context(nc.psum_tensor(nm("psG"), [P, 512], F32))
                psU = ls.enter_context(nc.psum_tensor(nm("psU"), [P, 512], F32))
                for t in range(NT):
                    load_xT(xT, XQT, t * P)
                    for b in range(6):
                        w = min(512, 2816 - b * 512)
                        for kc in range(8):
                            TE(lambda e, b=b, w=w, kc=kc: e.matmul(
                                psG[:, 0:w], xT[:, kc * P:(kc + 1) * P],
                                WGT[:, kc * 2816 + b * 512:kc * 2816 + b * 512 + w],
                                start=(kc == 0), stop=(kc == 7), skip_group_check=True))
                        for kc in range(8):
                            TE(lambda e, b=b, w=w, kc=kc: e.matmul(
                                psU[:, 0:w], xT[:, kc * P:(kc + 1) * P],
                                WUT[:, kc * 2816 + b * 512:kc * 2816 + b * 512 + w],
                                start=(kc == 0), stop=(kc == 7), skip_group_check=True))
                        S(lambda e, w=w: e.activation(gsb[:, 0:w], psG[:, 0:w],
                                                      ACT.Silu, scale=1.0))
                        V(lambda e, b=b, w=w: e.tensor_tensor(
                            Hf[:, b * 512:b * 512 + w], gsb[:, 0:w], psU[:, 0:w],
                            ALU.mult))
                    fp4_qdq(Hf[:], hqb, 2816, q)
                    for c in range(22):
                        tr128(psT6[:], hqb[:, c * P:(c + 1) * P], trb[:])
                        DMA(lambda e, t=t, c=c: e.dma_start(
                            HQT[c * P:(c + 1) * P, t * P:(t + 1) * P], trb[:]))
            with contextlib.ExitStack() as ls:
                lb = lambda n, sh, dt=F32: ls.enter_context(nc.sbuf_tensor(nm(n), sh, dt))
                WDT = lb("WDT", [P, 22 * D], BF16)
                q = mk_qbufs(ls, 2816)
                wstg = lb("wstg2", [P, 2816]); wqb = lb("wqb2", [P, 2816], BF16)
                trb = lb("trb8", [P, P], BF16)
                psT7 = ls.enter_context(nc.psum_tensor(nm("psT7"), [P, P], F32))
                for oc in range(8):
                    DMA(lambda e, oc=oc: e.dma_start(
                        wstg[:], w1_down[oc * P:(oc + 1) * P, :]))
                    fp4_qdq(wstg[:], wqb, 2816, q)
                    for ic in range(22):
                        tr128(psT7[:], wqb[:, ic * P:(ic + 1) * P], trb[:])
                        V(lambda e, ic=ic, oc=oc: e.tensor_copy(
                            WDT[:, ic * D + oc * P:ic * D + (oc + 1) * P], trb[:]))
                hT = lb("hT2", [P, 22 * P], BF16)
                of = lb("of2", [P, D])
                psO2 = ls.enter_context(nc.psum_tensor(nm("psO2"), [P, D], F32))
                for t in range(NT):
                    for kc in range(22):
                        DMA(lambda e, t=t, kc=kc: e.dma_start(
                            hT[:, kc * P:(kc + 1) * P],
                            HQT[kc * P:(kc + 1) * P, t * P:(t + 1) * P]))
                    dense(psO2, hT, WDT, D, kchunks=22)
                    V(lambda e: e.tensor_copy(of[:], psO2[:, 0:D]))
                    DMA(lambda e, t=t: e.dma_start(FOUT[t * P:(t + 1) * P, :], of[:]))

        def l2_ffn():
            with contextlib.ExitStack() as ls:
                lb = lambda n, sh, dt=F32: ls.enter_context(nc.sbuf_tensor(nm(n), sh, dt))
                WTT = lb("WTT", [P, 8 * D], BF16)
                pk = lb("pk", [P, D], I32)
                tm_i = lb("tm_i", [P, D], I32)
                tm_f = lb("tm_f", [P, D])
                wb = lb("wb2", [P, D], BF16)
                trb = lb("trb9", [P, P], BF16)
                psT8 = ls.enter_context(nc.psum_tensor(nm("psT8"), [P, P], F32))
                for half in range(2):
                    DMA(lambda e, half=half: e.dma_start(
                        pk[:], w2_pk[half * P:(half + 1) * P, :]))
                    for j in range(4):
                        oc = j * 2 + half
                        V(lambda e, j=j: e.tensor_scalar(
                            tm_i[:], pk[:], 2 * j, 3, ALU.logical_shift_right,
                            ALU.bitwise_and))
                        V(lambda e: e.tensor_copy(tm_f[:], tm_i[:]))
                        V(lambda e: e.tensor_scalar(tm_f[:], tm_f[:], -1.0, None,
                                                    ALU.add))
                        V(lambda e: e.tensor_copy(wb[:], tm_f[:]))
                        for ic in range(8):
                            tr128(psT8[:], wb[:, ic * P:(ic + 1) * P], trb[:])
                            V(lambda e, ic=ic, oc=oc: e.tensor_copy(
                                WTT[:, ic * D + oc * P:ic * D + (oc + 1) * P], trb[:]))
                scr = lb("scr", [P, 1])
                DMA(lambda e: e.dma_start(scr[:], w2_scale.broadcast_to([P, 1])))
                brep = lb("brep", [P, D])
                DMA(lambda e: e.dma_start(brep[:], b2_f.broadcast_to([P, D])))
                xt = lb("xt5", [P, D]); mx = lb("mx", [P, 1]); iv2 = lb("iv2", [P, 1])
                scomb = lb("scomb", [P, 1])
                xr = lb("xr", [P, D]); xrb = lb("xrb", [P, D], BF16)
                xqT = lb("xqT", [P, 8 * P], BF16)
                of = lb("of3", [P, D])
                psO3 = ls.enter_context(nc.psum_tensor(nm("psO3"), [P, D], F32))
                for t in range(NT):
                    DMA(lambda e, t=t: e.dma_start(xt[:], XN[t * P:(t + 1) * P, :]))
                    V(lambda e: e.tensor_reduce(mx[:], xt[:], AX.X, ALU.max,
                                                apply_absolute_value=True))
                    V(lambda e: e.tensor_scalar(mx[:], mx[:], 1e-5, None, ALU.max))
                    V(lambda e: e.reciprocal(iv2[:], mx[:]))
                    V(lambda e: e.tensor_scalar(xr[:], xt[:], iv2[:], 127.0,
                                                ALU.mult, ALU.mult))
                    V(lambda e: e.tensor_scalar(xr[:], xr[:], MAGIC, -MAGIC,
                                                ALU.add, ALU.add))
                    V(lambda e: e.tensor_copy(xrb[:], xr[:]))
                    for c in range(8):
                        tr128(psT8[:], xrb[:, c * P:(c + 1) * P],
                              xqT[:, c * P:(c + 1) * P])
                    dense(psO3, xqT, WTT, D)
                    V(lambda e: e.tensor_tensor(scomb[:], mx[:], scr[:], ALU.mult))
                    V(lambda e: e.tensor_scalar(scomb[:], scomb[:], 1.0 / 127.0, None,
                                                ALU.mult))
                    V(lambda e: e.tensor_scalar(of[:], psO3[:, 0:D], scomb[:], None,
                                                ALU.mult))
                    V(lambda e: e.tensor_tensor(of[:], of[:], brep[:], ALU.add))
                    DMA(lambda e, t=t: e.dma_start(FOUT[t * P:(t + 1) * P, :], of[:]))

        def gmax_allreduce(gacc, idx, grep, g1):
            DMA(lambda e: e.dma_start(GD, gacc[:]))
            DMA(lambda e: e.dma_start(g1[0:1, 0:P], GD.rearrange("p o -> o p"),
                                      allow_slow_non_contiguous=True))
            V(lambda e: e.tensor_reduce(g1[0:1, P:P + 1], g1[0:1, 0:P], AX.X, ALU.max))
            DMA(lambda e, idx=idx: e.dma_start(ARI[idx], g1[0:1, P:P + 1]))
            GP(lambda e, idx=idx: e.collective_compute(
                "AllReduce", ALU.max, replica_groups=[list(range(NCORE))],
                ins=[ARI[idx]], outs=[ARO[idx]]))
            DMA(lambda e, idx=idx: e.dma_start(grep[:], ARO[idx].broadcast_to([P, 1])))
            V(lambda e: e.tensor_scalar(grep[:], grep[:], 1.0 / 32767.0, 1e-12,
                                        ALU.mult, ALU.max))

        def aqdq(dst, src_ap, sinv, sc, tmp):
            V(lambda e: e.tensor_scalar(tmp, src_ap, sinv[:], None, ALU.mult))
            V(lambda e: e.tensor_scalar(tmp, tmp, MAGIC, -MAGIC, ALU.add, ALU.add))
            V(lambda e: e.tensor_scalar(dst, tmp, sc[:], None, ALU.mult))

        def l3_ffn():
            with contextlib.ExitStack() as ls:
                lb = lambda n, sh, dt=F32: ls.enter_context(nc.sbuf_tensor(nm(n), sh, dt))
                gacc = lb("gacc", [P, 1]); tmax = lb("tmax", [P, 1])
                g1 = lb("g1", [1, P + 1])
                grx = lb("grx", [P, 1]); gix = lb("gix", [P, 1])
                gru = lb("gru", [P, 1]); giu = lb("giu", [P, 1])
                grg = lb("grg", [P, 1]); gig = lb("gig", [P, 1])
                gr2 = lb("gr2", [P, 1]); gi2 = lb("gi2", [P, 1])
                gro = lb("gro", [P, 1])
                xt = lb("xt6", [P, 3072]); tmpq = lb("tmpq", [P, 3072])
                xqb = lb("xqb2", [P, D], BF16)
                trb = lb("trbA", [P, P], BF16)
                psT9 = ls.enter_context(nc.psum_tensor(nm("psT9"), [P, P], F32))
                V(lambda e: e.memset(gacc[:], 0.0))
                for t in range(NT):
                    DMA(lambda e, t=t: e.dma_start(xt[:, 0:D], XN[t * P:(t + 1) * P, :]))
                    V(lambda e: e.tensor_reduce(tmax[:], xt[:, 0:D], AX.X, ALU.max,
                                                apply_absolute_value=True))
                    V(lambda e: e.tensor_tensor(gacc[:], gacc[:], tmax[:], ALU.max))
                gmax_allreduce(gacc, 0, grx, g1)
                V(lambda e: e.reciprocal(gix[:], grx[:]))
                WUT3 = lb("WUT3", [P, 8 * 3072], BF16)
                WGT3 = lb("WGT3", [P, 8 * 3072], BF16)
                q = mk_qbufs(ls, D)
                wstg = lb("wstg3", [P, D]); wqb = lb("wqb3", [P, D], BF16)
                for wsrc, Wdst in ((w3_up, WUT3), (w3_gate, WGT3)):
                    for oc in range(24):
                        DMA(lambda e, oc=oc, wsrc=wsrc: e.dma_start(
                            wstg[:], wsrc[oc * P:(oc + 1) * P, :]))
                        lpbq_qdq(wstg[:], wqb, D, q)
                        for ic in range(8):
                            tr128(psT9[:], wqb[:, ic * P:(ic + 1) * P], trb[:])
                            V(lambda e, ic=ic, oc=oc, Wdst=Wdst: e.tensor_copy(
                                Wdst[:, ic * 3072 + oc * P:ic * 3072 + (oc + 1) * P],
                                trb[:]))
                xT = lb("xT6", [P, 8 * P], BF16)
                upf = lb("upf", [P, 3072])
                gaccu = lb("gaccu", [P, 1]); gaccg = lb("gaccg", [P, 1])
                V(lambda e: e.memset(gaccu[:], 0.0))
                V(lambda e: e.memset(gaccg[:], 0.0))
                psM = ls.enter_context(nc.psum_tensor(nm("psM"), [P, 512], F32))
                for t in range(NT):
                    DMA(lambda e, t=t: e.dma_start(xt[:, 0:D], XN[t * P:(t + 1) * P, :]))
                    aqdq(xt[:, 0:D], xt[:, 0:D], gix, grx, tmpq[:, 0:D])
                    V(lambda e: e.tensor_copy(xqb[:], xt[:, 0:D]))
                    for c in range(8):
                        tr128(psT9[:], xqb[:, c * P:(c + 1) * P],
                              xT[:, c * P:(c + 1) * P])
                    for Wd, dstH, ga in ((WUT3, UPH, gaccu), (WGT3, GTH, gaccg)):
                        for b in range(6):
                            for kc in range(8):
                                TE(lambda e, b=b, kc=kc, Wd=Wd: e.matmul(
                                    psM[:, 0:512], xT[:, kc * P:(kc + 1) * P],
                                    Wd[:, kc * 3072 + b * 512:kc * 3072 + (b + 1) * 512],
                                    start=(kc == 0), stop=(kc == 7),
                                    skip_group_check=True))
                            V(lambda e, b=b: e.tensor_copy(
                                upf[:, b * 512:(b + 1) * 512], psM[:, 0:512]))
                        V(lambda e: e.tensor_reduce(tmax[:], upf[:], AX.X, ALU.max,
                                                    apply_absolute_value=True))
                        V(lambda e, ga=ga: e.tensor_tensor(ga[:], ga[:], tmax[:],
                                                           ALU.max))
                        DMA(lambda e, t=t, dstH=dstH: e.dma_start(
                            dstH[t * P:(t + 1) * P, :], upf[:]))
                gmax_allreduce(gaccu, 1, gru, g1)
                V(lambda e: e.reciprocal(giu[:], gru[:]))
                gmax_allreduce(gaccg, 2, grg, g1)
                V(lambda e: e.reciprocal(gig[:], grg[:]))
                sgb = lb("sgb", [P, 3072])
                V(lambda e: e.memset(gacc[:], 0.0))
                for t in range(NT):
                    DMA(lambda e, t=t: e.dma_start(xt[:], GTH[t * P:(t + 1) * P, :]))
                    aqdq(xt[:], xt[:], gig, grg, tmpq[:])
                    S(lambda e: e.activation(sgb[:], xt[:], ACT.Sigmoid, scale=1.0))
                    V(lambda e: e.tensor_scalar(sgb[:], sgb[:], 65536.0, MAGIC,
                                                ALU.mult, ALU.add))
                    V(lambda e: e.tensor_scalar(sgb[:], sgb[:], -MAGIC, 65535.0,
                                                ALU.add, ALU.min))
                    V(lambda e: e.tensor_scalar(sgb[:], sgb[:], 1.0 / 65536.0, None,
                                                ALU.mult))
                    V(lambda e: e.tensor_tensor(xt[:], xt[:], sgb[:], ALU.mult))
                    V(lambda e: e.tensor_reduce(tmax[:], xt[:], AX.X, ALU.max,
                                                apply_absolute_value=True))
                    V(lambda e: e.tensor_tensor(gacc[:], gacc[:], tmax[:], ALU.max))
                    DMA(lambda e, t=t: e.dma_start(G2H[t * P:(t + 1) * P, :], xt[:]))
                    DMA(lambda e, t=t: e.dma_start(xt[:], UPH[t * P:(t + 1) * P, :]))
                    aqdq(xt[:], xt[:], giu, gru, tmpq[:])
                    DMA(lambda e, t=t: e.dma_start(UQH[t * P:(t + 1) * P, :], xt[:]))
                gmax_allreduce(gacc, 3, gr2, g1)
                V(lambda e: e.reciprocal(gi2[:], gr2[:]))
                V(lambda e: e.memset(gaccu[:], 0.0))
                for t in range(NT):
                    DMA(lambda e, t=t: e.dma_start(xt[:], G2H[t * P:(t + 1) * P, :]))
                    aqdq(xt[:], xt[:], gi2, gr2, tmpq[:])
                    DMA(lambda e, t=t: e.dma_start(tmpq[:], UQH[t * P:(t + 1) * P, :]))
                    V(lambda e: e.tensor_tensor(xt[:], xt[:], tmpq[:], ALU.mult))
                    V(lambda e: e.tensor_reduce(tmax[:], xt[:], AX.X, ALU.max,
                                                apply_absolute_value=True))
                    V(lambda e: e.tensor_tensor(gaccu[:], gaccu[:], tmax[:], ALU.max))
                    DMA(lambda e, t=t: e.dma_start(OH[t * P:(t + 1) * P, :], xt[:]))
                gmax_allreduce(gaccu, 4, gro, g1)
            with contextlib.ExitStack() as ls:
                lb = lambda n, sh, dt=F32: ls.enter_context(nc.sbuf_tensor(nm(n), sh, dt))
                WDT3 = lb("WDT3", [P, 24 * D], BF16)
                q = mk_qbufs(ls, 3072)
                wstg = lb("wstg4", [P, 3072]); wqb = lb("wqb4", [P, 3072], BF16)
                trb = lb("trbB", [P, P], BF16)
                gro2 = lb("gro2", [P, 1]); gio2 = lb("gio2", [P, 1])
                DMA(lambda e: e.dma_start(gro2[:], ARO[4].broadcast_to([P, 1])))
                V(lambda e: e.tensor_scalar(gro2[:], gro2[:], 1.0 / 32767.0, 1e-12,
                                            ALU.mult, ALU.max))
                V(lambda e: e.reciprocal(gio2[:], gro2[:]))
                psTA = ls.enter_context(nc.psum_tensor(nm("psTA"), [P, P], F32))
                for oc in range(8):
                    DMA(lambda e, oc=oc: e.dma_start(
                        wstg[:], w3_down[oc * P:(oc + 1) * P, :]))
                    lpbq_qdq(wstg[:], wqb, 3072, q)
                    for ic in range(24):
                        tr128(psTA[:], wqb[:, ic * P:(ic + 1) * P], trb[:])
                        V(lambda e, ic=ic, oc=oc: e.tensor_copy(
                            WDT3[:, ic * D + oc * P:ic * D + (oc + 1) * P], trb[:]))
                xt = lb("xt7", [P, 3072]); tmpq = lb("tmpq2", [P, 3072])
                oqb = lb("oqb", [P, 3072], BF16)
                oT = lb("oT", [P, 24 * P], BF16)
                of = lb("of4", [P, D])
                psO4 = ls.enter_context(nc.psum_tensor(nm("psO4"), [P, D], F32))
                for t in range(NT):
                    DMA(lambda e, t=t: e.dma_start(xt[:], OH[t * P:(t + 1) * P, :]))
                    aqdq(xt[:], xt[:], gio2, gro2, tmpq[:])
                    V(lambda e: e.tensor_copy(oqb[:], xt[:]))
                    for c in range(24):
                        tr128(psTA[:], oqb[:, c * P:(c + 1) * P],
                              oT[:, c * P:(c + 1) * P])
                    dense(psO4, oT, WDT3, D, kchunks=24)
                    V(lambda e: e.tensor_copy(of[:], psO4[:, 0:D]))
                    DMA(lambda e, t=t: e.dma_start(FOUT[t * P:(t + 1) * P, :], of[:]))

        def update(final, first=False):
            with contextlib.ExitStack() as ls:
                lb = lambda n, sh, dt=F32: ls.enter_context(nc.sbuf_tensor(nm(n), sh, dt))
                a = lb("ua", [P, D]); b = lb("ub", [P, D]); c = lb("uc", [P, D])
                srcr = x_in if first else X
                for t in range(NT):
                    DMA(lambda e, t=t, srcr=srcr: e.dma_start(
                        a[:], srcr[t * P:(t + 1) * P, :]))
                    DMA(lambda e, t=t: e.dma_start(b[:], AOUT[t * P:(t + 1) * P, :]))
                    DMA(lambda e, t=t: e.dma_start(c[:], FOUT[t * P:(t + 1) * P, :]))
                    V(lambda e: e.tensor_tensor(a[:], a[:], b[:], ALU.add))
                    V(lambda e: e.tensor_tensor(a[:], a[:], c[:], ALU.add))
                    dst = out_ext if final else X
                    DMA(lambda e, t=t, dst=dst: e.dma_start(
                        dst[t * P:(t + 1) * P, :], a[:]))

        def ckpt(src):
            with contextlib.ExitStack() as ls:
                buf = ls.enter_context(nc.sbuf_tensor(nm("ck"), [P, D], F32))
                for t in range(NT):
                    DMA(lambda e, t=t: e.dma_start(buf[:], src[t * P:(t + 1) * P, :]))
                    DMA(lambda e, t=t: e.dma_start(
                        out_ext[t * P:(t + 1) * P, :], buf[:]))

        stages = [
            ("ln0", lambda: ln_stage(x_in), XN), ("a0", l0_attn, AOUT), ("f0", l0_ffn, FOUT),
            ("u0", lambda: update(False, True), X),
            ("ln1", ln_stage, XN), ("a1", l1_attn, AOUT), ("f1", l1_ffn, FOUT),
            ("u1", lambda: update(False), X),
            ("ln2", ln_stage, XN), ("a2", lambda: l23_attn(2), AOUT),
            ("f2", l2_ffn, FOUT), ("u2", lambda: update(False), X),
            ("ln3", ln_stage, XN), ("a3", lambda: l23_attn(3), AOUT),
            ("f3", l3_ffn, FOUT), ("u3", lambda: update(True), None),
        ]
        for name, fn, dump in stages:
            fn()
            if dbg is not None and name == dbg:
                if dump is not None:
                    ckpt(dump)
                break

        if st8["pend"] is not None:
            st8["pend"].then_inc(sem, 1)
            st8["cnt"] += 1
            st8["pend"] = None
        nc.sync.wait_ge(sem, st8["cnt"])
    return nc


_NC = None


def kernel(**inputs):
    global _NC
    f = lambda k: np.ascontiguousarray(np.asarray(inputs[k], np.float32))
    x = f("L0_attn_x")
    t = np.arange(2048, dtype=np.float64)
    invf = 1.0 / (10000.0 ** (np.arange(0, 64, 2, dtype=np.float64) / 64))
    emb = np.repeat(t[:, None] * invf[None, :], 2, axis=-1)
    cos1 = np.cos(emb).astype(np.float32)
    sin1 = np.sin(emb).astype(np.float32)
    costab = np.tile(cos1, (1, 20)).astype(np.float32)
    sintab = np.tile(sin1, (1, 20)).astype(np.float32)
    masks = np.concatenate([
        np.where(np.arange(512)[None, :] >= (j * P + np.arange(P))[:, None],
                 0.0, NEG).astype(np.float32)
        for j in range(4)], axis=0)
    common = {
        "w0_attn": f("L0_attn_w_attn"), "w0_proj": f("L0_attn_w_proj"),
        "w0_f1": f("L0_ffn_w1"), "b0_f1": f("L0_ffn_b1").reshape(32, 128),
        "w0_f2": f("L0_ffn_w2"), "b0_f2": f("L0_ffn_b2").reshape(1, -1),
        "w1_inp": np.ascontiguousarray(f("L1_attn_in_proj_weight").T),
        "b1_inp": f("L1_attn_in_proj_bias").reshape(1, -1),
        "w1_outp": np.ascontiguousarray(f("L1_attn_out_proj_weight").T),
        "b1_outp": f("L1_attn_out_proj_bias").reshape(1, -1),
        "w1_gate": f("L1_ffn_w_gate"), "w1_up": f("L1_ffn_w_up"),
        "w1_down": f("L1_ffn_w_down"),
        "w2_qkv": np.ascontiguousarray(np.concatenate(
            [f("L2_attn_Wq").T, f("L2_attn_Wk").T, f("L2_attn_Wv").T], axis=1)),
        "b2_qkv": np.concatenate(
            [f("L2_attn_bq"), f("L2_attn_bk"), f("L2_attn_bv")]).reshape(1, -1),
        "w2_o": np.ascontiguousarray(f("L2_attn_Wo").T),
        "b2_o": f("L2_attn_bo").reshape(1, -1),
        "w2_pk": np.ascontiguousarray(
            np.asarray(inputs["L2_ffn_packed_weight"], np.int32).reshape(256, 1024)),
        "w2_scale": f("L2_ffn_weight_scale").reshape(1, 1),
        "b2_f": f("L2_ffn_bias").reshape(1, -1),
        "w3_qkv": np.ascontiguousarray(np.concatenate(
            [f("L3_attn_Wq"), f("L3_attn_Wk"), f("L3_attn_Wv")], axis=1)),
        "b3_qkv": np.concatenate(
            [f("L3_attn_bq"), f("L3_attn_bk"), f("L3_attn_bv")]).reshape(1, -1),
        "w3_l": f("L3_attn_Wl"), "b3_l": f("L3_attn_bl").reshape(1, -1),
        "w3_o": f("L3_attn_Wo"), "b3_o": f("L3_attn_bo").reshape(1, -1),
        "w3_gate": f("L3_ffn_w_gate"), "w3_up": f("L3_ffn_w_up"),
        "w3_down": f("L3_ffn_w_down"),
        "ident": np.eye(P, dtype=np.float32),
        "costab": costab, "sintab": sintab, "masks": masks,
    }
    in_maps = []
    for c in range(NCORE):
        m = dict(common)
        m["x_in"] = np.ascontiguousarray(x[2 * c:2 * c + 2].reshape(M, D))
        in_maps.append(m)
    import os
    _dbg = os.environ.get("KDBG") or None
    if _dbg:
        prog = build(_dbg)
    else:
        if _NC is None:
            _NC = build()
        prog = _NC
    res = run_bass_kernel_spmd(prog, in_maps, core_ids=list(range(NCORE)))
    outs = [np.asarray(res.results[c]["out"], np.float32) for c in range(NCORE)]
    return np.concatenate(outs, axis=0).reshape(16, 2048, 1024)


# version 04 (final)
# speedup vs baseline: 1.0960x; 1.0960x over previous
import contextlib
import numpy as np
import ml_dtypes
import concourse.bass as bass
import concourse.mybir as mybir
from concourse.bass_utils import run_bass_kernel_spmd
from concourse.replica_groups import maybe_share_collective_output_space

F32 = mybir.dt.float32
BF16 = mybir.dt.bfloat16
I32 = mybir.dt.int32
ALU = mybir.AluOpType
ACT = mybir.ActivationFunctionType
AX = mybir.AxisListType
P = 128
NCORE = 8
M = 4096
NT = 32
D = 1024
MAGIC = 12582912.0
NEG = -1e30


def build(dbg=None):
    nc = bass.Bass("TRN2", target_bir_lowering=False)
    dr = lambda name, shape, dt=F32, kind="ExternalInput", sp=None: nc.dram_tensor(
        name, shape, dt, kind=kind, **({"addr_space": sp} if sp else {})).ap()

    x_in = dr("x_in", [M, D])
    w0_attn = dr("w0_attn", [D, 1536])
    w0_proj = dr("w0_proj", [D, D])
    w0_f1 = dr("w0_f1", [D, 4096])
    b0_f1 = dr("b0_f1", [32, 128])
    w0_f2 = dr("w0_f2", [4096, D])
    b0_f2 = dr("b0_f2", [1, D])
    w1_inp = dr("w1_inp", [D, 3072])
    b1_inp = dr("b1_inp", [1, 3072])
    w1_outp = dr("w1_outp", [D, D])
    b1_outp = dr("b1_outp", [1, D])
    w1_gate = dr("w1_gate", [2816, D])
    w1_up = dr("w1_up", [2816, D])
    w1_down = dr("w1_down", [D, 2816])
    w2_qkv = dr("w2_qkv", [D, 3072])
    b2_qkv = dr("b2_qkv", [1, 3072])
    w2_o = dr("w2_o", [D, D])
    b2_o = dr("b2_o", [1, D])
    w2_pk = dr("w2_pk", [256, D], I32)
    w2_scale = dr("w2_scale", [1, 1])
    b2_f = dr("b2_f", [1, D])
    w3_qkv = dr("w3_qkv", [D, 3072])
    b3_qkv = dr("b3_qkv", [1, 3072])
    w3_l = dr("w3_l", [D, D])
    b3_l = dr("b3_l", [1, D])
    w3_o = dr("w3_o", [D, D])
    b3_o = dr("b3_o", [1, D])
    w3_gate = dr("w3_gate", [3072, D])
    w3_up = dr("w3_up", [3072, D])
    w3_down = dr("w3_down", [D, 3072])
    ident = dr("ident", [P, P])
    costab = dr("costab", [2048, 1280])
    sintab = dr("sintab", [2048, 1280])
    masks = dr("masks", [4 * P, 512])

    out_ext = dr("out", [M, D], kind="ExternalOutput")

    X = dr("X", [M, D], kind="Internal")
    XN = dr("XN", [M, D], kind="Internal")
    XNT = dr("XNT", [D, M], BF16, kind="Internal")
    AOUT = dr("AOUT", [M, D], kind="Internal")
    FOUT = dr("FOUT", [M, D], kind="Internal")
    QL = dr("QL", [M, D], BF16, kind="Internal")
    KL = dr("KL", [M, D], BF16, kind="Internal")
    VL = dr("VL", [M, D], BF16, kind="Internal")
    spa = maybe_share_collective_output_space("AllGather", [list(range(NCORE))])
    KG = dr("KG", [NCORE * M, D], BF16, kind="Internal", sp=spa)
    VG = dr("VG", [NCORE * M, D], BF16, kind="Internal", sp=spa)
    HT = dr("HT", [4096, M], BF16, kind="Internal")
    XQT = dr("XQT", [D, M], BF16, kind="Internal")
    HQT = dr("HQT", [2816, M], BF16, kind="Internal")
    QTH = dr("QTH", [D, 2 * 2048], BF16, kind="Internal")
    KTH = dr("KTH", [D, 2 * 2048], BF16, kind="Internal")
    V65H = dr("V65H", [2 * P, 16 * 1040], BF16, kind="Internal")
    UPH = dr("UPH", [M, 3072], kind="Internal")
    GTH = dr("GTH", [M, 3072], kind="Internal")
    G2H = dr("G2H", [M, 3072], kind="Internal")
    UQH = dr("UQH", [M, 3072], kind="Internal")
    OH = dr("OH", [M, 3072], kind="Internal")
    GD = dr("GD", [P, 1], kind="Internal")
    spr = maybe_share_collective_output_space("AllReduce", [list(range(NCORE))])
    ARI = [dr(f"ari{i}", [1, 1], kind="Internal") for i in range(5)]
    ARO = [dr(f"aro{i}", [1, 1], kind="Internal", sp=spr) for i in range(5)]

    es = contextlib.ExitStack()
    sb = lambda n, sh, dt=F32: es.enter_context(nc.sbuf_tensor(n, sh, dt))
    uid = [0]

    def nm(pfx):
        uid[0] += 1
        return f"{pfx}{uid[0]}"

    with es:
        sem = es.enter_context(nc.semaphore("seq"))
        st8 = {"eng": None, "pend": None, "cnt": 0}

        def run(engname, fn):
            multi = engname in ("vector", "scalar", "sync", "gpsimd")
            if engname != st8["eng"] or multi:
                if st8["pend"] is not None:
                    st8["pend"].then_inc(sem, 1)
                    st8["cnt"] += 1
                    st8["pend"] = None
                if st8["cnt"]:
                    getattr(nc, engname).wait_ge(sem, st8["cnt"])
                st8["eng"] = engname
            inst = fn(getattr(nc, engname))
            if engname == "tensor":
                st8["pend"] = inst
            else:
                inc = 16 if engname == "sync" else 1
                inst.then_inc(sem, inc)
                st8["cnt"] += inc

        V = lambda fn: run("vector", fn)
        S = lambda fn: run("scalar", fn)
        TE = lambda fn: run("tensor", fn)
        DMA = lambda fn: run("sync", fn)
        GP = lambda fn: run("gpsimd", fn)

        idf = sb("idf", [P, P])
        idb = sb("idb", [P, P], BF16)
        onesb = sb("onesb", [1, P], BF16)
        eps5 = sb("eps5", [P, 1])
        eps6 = sb("eps6", [P, 1])
        DMA(lambda e: e.dma_start(idf[:], ident))
        V(lambda e: e.tensor_copy(idb[:], idf[:]))
        V(lambda e: e.memset(onesb[:], 1.0))
        V(lambda e: e.memset(eps5[:], 1e-5))
        V(lambda e: e.memset(eps6[:], 1e-6))

        def tr128(psA, src_bf, dst_bf):
            TE(lambda e: e.matmul(psA, src_bf, idb[:], start=True, stop=True,
                                  skip_group_check=True))
            V(lambda e: e.tensor_copy(dst_bf, psA))

        def load_w_bf(dst_sb, src_dram, kchunks, ncols, stg):
            for kc in range(kchunks):
                nsub = (ncols + 511) // 512
                for sbi in range(nsub):
                    w = min(512, ncols - sbi * 512)
                    DMA(lambda e, kc=kc, sbi=sbi, w=w: e.dma_start(
                        stg[:, 0:w],
                        src_dram[kc * P:(kc + 1) * P, sbi * 512:sbi * 512 + w]))
                    V(lambda e, kc=kc, sbi=sbi, w=w: e.tensor_copy(
                        dst_sb[:, kc * ncols + sbi * 512:kc * ncols + sbi * 512 + w],
                        stg[:, 0:w]))

        def load_bias_bf(dst_sb, src_dram, ncols, stg):
            nsub = (ncols + 511) // 512
            for sbi in range(nsub):
                w = min(512, ncols - sbi * 512)
                DMA(lambda e, sbi=sbi, w=w: e.dma_start(
                    stg[0:1, 0:w], src_dram[0:1, sbi * 512:sbi * 512 + w]))
                V(lambda e, sbi=sbi, w=w: e.tensor_copy(
                    dst_sb[0:1, sbi * 512:sbi * 512 + w], stg[0:1, 0:w]))

        def dense(ps, xT_sb, W_sb, N, bias_sb=None, kchunks=8):
            nb = (N + 511) // 512
            for b in range(nb):
                w = min(512, N - b * 512)
                for kc in range(kchunks):
                    TE(lambda e, b=b, w=w, kc=kc: e.matmul(
                        ps[:, b * 512:b * 512 + w],
                        xT_sb[:, kc * P:(kc + 1) * P],
                        W_sb[:, kc * N + b * 512:kc * N + b * 512 + w],
                        start=(kc == 0), stop=(kc == kchunks - 1 and bias_sb is None),
                        skip_group_check=True))
                if bias_sb is not None:
                    TE(lambda e, b=b, w=w: e.matmul(
                        ps[:, b * 512:b * 512 + w], onesb[:],
                        bias_sb[0:1, b * 512:b * 512 + w],
                        start=False, stop=True, skip_group_check=True))

        def load_xT(dst_sb, srcT, col0, kchunks=8):
            DMA(lambda e: e.dma_start(
                dst_sb[:, 0:kchunks * P].rearrange("p (k n) -> p k n", n=P),
                srcT[0:kchunks * P, col0:col0 + P]
                .rearrange("(k p) n -> p k n", p=P)))

        def ln_stage(src_X=X):
            with contextlib.ExitStack() as ls:
                lb = lambda n, sh, dt=F32: ls.enter_context(nc.sbuf_tensor(nm(n), sh, dt))
                xt = lb("xt", [P, D]); sq = lb("sq", [P, D])
                xnb = lb("xnb", [P, D], BF16); trb = lb("trb", [P, P], BF16)
                mB = lb("m", [P, 1]); ss = lb("ss", [P, 1]); vr = lb("vr", [P, 1])
                iv = lb("iv", [P, 1])
                pst = ls.enter_context(nc.psum_tensor(nm("pst"), [P, P], F32))
                for t in range(NT):
                    DMA(lambda e, t=t: e.dma_start(xt[:], src_X[t * P:(t + 1) * P, :]))
                    V(lambda e: e.tensor_reduce(mB[:], xt[:], AX.X, ALU.add))
                    V(lambda e: e.tensor_scalar(mB[:], mB[:], 1.0 / D, None, ALU.mult))
                    V(lambda e: e.tensor_tensor(sq[:], xt[:], xt[:], ALU.mult))
                    V(lambda e: e.tensor_reduce(ss[:], sq[:], AX.X, ALU.add))
                    V(lambda e: e.tensor_scalar(ss[:], ss[:], 1.0 / D, None, ALU.mult))
                    V(lambda e: e.tensor_tensor(vr[:], mB[:], mB[:], ALU.mult))
                    V(lambda e: e.tensor_tensor(vr[:], ss[:], vr[:], ALU.subtract))
                    S(lambda e: e.activation(vr[:], vr[:], ACT.Sqrt, bias=eps5[:],
                                             scale=1.0))
                    V(lambda e: e.reciprocal(iv[:], vr[:]))
                    V(lambda e: e.tensor_scalar(xt[:], xt[:], mB[:], iv[:],
                                                ALU.subtract, ALU.mult))
                    DMA(lambda e, t=t: e.dma_start(XN[t * P:(t + 1) * P, :], xt[:]))
                    V(lambda e: e.tensor_copy(xnb[:], xt[:]))
                    for c in range(8):
                        tr128(pst[:], xnb[:, c * P:(c + 1) * P], trb[:])
                        DMA(lambda e, t=t, c=c: e.dma_start(
                            XNT[c * P:(c + 1) * P, t * P:(t + 1) * P], trb[:]))

        def attn_core(r, QT_sb, KT_of, V65_sb, v65row, Wo_sb, bo_sb, causal, mlist):
            with contextlib.ExitStack() as ls:
                lb = lambda n, sh, dt=F32: ls.enter_context(nc.sbuf_tensor(nm(n), sh, dt))
                E = lb("E", [P, 512], BF16)
                mtmp = lb("mtmp", [P, 512])
                yf = lb("yf", [P, 65]); rc = lb("rc", [P, 1])
                CT = lb("CT", [P, 4 * D])
                ctb = lb("ctb", [P, D], BF16)
                ctT = lb("ctT", [P, 8 * P], BF16)
                pj_f = lb("pjf", [P, D])
                psS = ls.enter_context(nc.psum_tensor(nm("psS"), [P, 512], F32))
                psY = [ls.enter_context(nc.psum_tensor(nm("psY"), [P, 65], F32))
                       for _ in range(4)]
                psT = ls.enter_context(nc.psum_tensor(nm("psT"), [P, P], F32))
                psJ = ls.enter_context(nc.psum_tensor(nm("psJ"), [P, D], F32))
                for qb in range(4):
                    nsc = qb * 4 + 4 if causal else 16
                    for h in range(16):
                        off = (h % 2) * 64
                        qc = h // 2
                        kbuf, kbase = KT_of(h)
                        kvi = h if v65row == 1040 else h // 4
                        for sc in range(nsc):
                            TE(lambda e, sc=sc, qb=qb, off=off, qc=qc,
                               kbuf=kbuf, kbase=kbase: e.matmul(
                                   psS[:, 0:512],
                                   kbuf[off:off + 64,
                                        kbase + sc * P:kbase + (sc + 1) * P],
                                   QT_sb[off:off + 64,
                                         qc * 2048 + qb * 512:qc * 2048 + qb * 512 + 512],
                                   start=True, stop=True, tile_position=(off, 0),
                                   skip_group_check=True))
                            j = sc - qb * 4
                            if causal and j >= 0:
                                V(lambda e, j=j: e.tensor_tensor(
                                    mtmp[:], psS[:, 0:512], mlist[j], ALU.add))
                                S(lambda e: e.activation(E[:], mtmp[:], ACT.Exp,
                                                         scale=0.125))
                            else:
                                S(lambda e: e.activation(E[:], psS[:, 0:512], ACT.Exp,
                                                         scale=0.125))
                            for qs in range(4):
                                TE(lambda e, qs=qs, sc=sc, kvi=kvi, nsc=nsc: e.matmul(
                                    psY[qs][:, 0:65],
                                    E[:, qs * P:(qs + 1) * P],
                                    V65_sb[:, sc * v65row + kvi * 65:
                                           sc * v65row + kvi * 65 + 65],
                                    start=(sc == 0), stop=(sc == nsc - 1),
                                    skip_group_check=True))
                        for qs in range(4):
                            V(lambda e, qs=qs: e.tensor_copy(yf[:], psY[qs][:, 0:65]))
                            V(lambda e: e.reciprocal(rc[:], yf[:, 64:65]))
                            V(lambda e, qs=qs, h=h: e.tensor_scalar(
                                CT[:, qs * D + h * 64:qs * D + h * 64 + 64],
                                yf[:, 0:64], rc[:], None, ALU.mult))
                    for qs in range(4):
                        V(lambda e, qs=qs: e.tensor_copy(ctb[:], CT[:, qs * D:(qs + 1) * D]))
                        for c in range(8):
                            tr128(psT[:], ctb[:, c * P:(c + 1) * P],
                                  ctT[:, c * P:(c + 1) * P])
                        dense(psJ, ctT, Wo_sb, D, bias_sb=bo_sb)
                        V(lambda e: e.tensor_copy(pj_f[:], psJ[:, 0:D]))
                        t_g = r * 16 + qb * 4 + qs
                        DMA(lambda e, t_g=t_g: e.dma_start(
                            AOUT[t_g * P:(t_g + 1) * P, :], pj_f[:]))

        def rope(qk, rot, cos_t, sin_t, c0, W):
            V(lambda e: e.tensor_scalar(
                rot[:, 0:W].rearrange("p (a two) -> p a two", two=2)[:, :, 0],
                qk[:, c0:c0 + W].rearrange("p (a two) -> p a two", two=2)[:, :, 1],
                -1.0, None, ALU.mult))
            V(lambda e: e.tensor_copy(
                rot[:, 0:W].rearrange("p (a two) -> p a two", two=2)[:, :, 1],
                qk[:, c0:c0 + W].rearrange("p (a two) -> p a two", two=2)[:, :, 0]))
            V(lambda e: e.tensor_tensor(qk[:, c0:c0 + W], qk[:, c0:c0 + W],
                                        cos_t[:, 0:W], ALU.mult))
            V(lambda e: e.tensor_tensor(rot[:, 0:W], rot[:, 0:W],
                                        sin_t[:, 0:W], ALU.mult))
            V(lambda e: e.tensor_tensor(qk[:, c0:c0 + W], qk[:, c0:c0 + W],
                                        rot[:, 0:W], ALU.add))

        def l0_attn():
            with contextlib.ExitStack() as ls:
                lb = lambda n, sh, dt=F32: ls.enter_context(nc.sbuf_tensor(nm(n), sh, dt))
                W_sb = lb("w0a", [P, 8 * 1536], BF16)
                Wo_sb = lb("w0p", [P, 8 * D], BF16)
                stg = lb("stg", [P, 512])
                load_w_bf(W_sb, w0_attn, 8, 1536, stg)
                load_w_bf(Wo_sb, w0_proj, 8, D, stg)
                msk0 = lb("msk", [P, 4 * 512])
                DMA(lambda e: e.dma_start(
                    msk0[:].rearrange("p (j w) -> p j w", j=4),
                    masks.rearrange("(j p) w -> p j w", j=4)))
                mlist = [msk0[:, j * 512:(j + 1) * 512] for j in range(4)]
                for r in range(2):
                    with contextlib.ExitStack() as rs:
                        rb = lambda n, sh, dt=F32: rs.enter_context(
                            nc.sbuf_tensor(nm(n), sh, dt))
                        QT = rb("QT", [P, 8 * 2048], BF16)
                        KD = [rb("KD", [P, 2048], BF16) for _ in range(4)]
                        V65 = rb("V65", [P, 16 * 260], BF16)
                        V(lambda e: e.memset(V65[:], 1.0))
                        with contextlib.ExitStack() as qs_st:
                            qb_ = lambda n, sh, dt=F32: qs_st.enter_context(
                                nc.sbuf_tensor(nm(n), sh, dt))
                            xT = qb_("xT", [P, 8 * P], BF16)
                            qk = qb_("qk", [P, 1536])
                            rot = qb_("rot", [P, 1280])
                            cos_t = qb_("cos", [P, 1280]); sin_t = qb_("sin", [P, 1280])
                            gs = qb_("gs", [P, 20]); sq2 = qb_("sq2", [P, 1280])
                            qkb = qb_("qkb", [P, 1536], BF16)
                            trb = qb_("trb3", [P, P], BF16)
                            psQ = qs_st.enter_context(
                                nc.psum_tensor(nm("psQ"), [P, 1536], F32))
                            psT2 = qs_st.enter_context(
                                nc.psum_tensor(nm("psT2"), [P, P], F32))
                            for stt in range(16):
                                load_xT(xT, XNT, r * 2048 + stt * P)
                                dense(psQ, xT, W_sb, 1536)
                                V(lambda e: e.tensor_copy(qk[:], psQ[:, 0:1536]))
                                DMA(lambda e, stt=stt: e.dma_start(
                                    cos_t[:], costab[stt * P:(stt + 1) * P, :]))
                                DMA(lambda e, stt=stt: e.dma_start(
                                    sin_t[:], sintab[stt * P:(stt + 1) * P, :]))
                                rope(qk, rot, cos_t, sin_t, 0, 1280)
                                V(lambda e: e.tensor_tensor(sq2[:], qk[:, 0:1280],
                                                            qk[:, 0:1280], ALU.mult))
                                V(lambda e: e.tensor_reduce(
                                    gs[:], sq2[:].rearrange("p (g d) -> p g d", d=64),
                                    AX.X, ALU.add))
                                S(lambda e: e.activation(gs[:], gs[:], ACT.Sqrt,
                                                         bias=eps6[:], scale=1.0 / 64))
                                V(lambda e: e.reciprocal(gs[:], gs[:]))
                                V(lambda e: e.tensor_tensor(
                                    qk[:, 0:1280].rearrange("p (g d) -> p g d", d=64),
                                    qk[:, 0:1280].rearrange("p (g d) -> p g d", d=64),
                                    gs[:].rearrange("p (g o) -> p g o", o=1)
                                    .broadcast_to([P, 20, 64]), ALU.mult))
                                V(lambda e: e.tensor_copy(qkb[:, 0:1280], qk[:, 0:1280]))
                                V(lambda e: e.tensor_copy(qkb[:, 1280:1536],
                                                          psQ[:, 1280:1536]))
                                for c in range(8):
                                    tr128(psT2[:], qkb[:, c * P:(c + 1) * P], trb[:])
                                    V(lambda e, c=c, stt=stt: e.tensor_copy(
                                        QT[:, c * 2048 + stt * P:c * 2048 + (stt + 1) * P],
                                        trb[:]))
                                for kv in range(4):
                                    tr128(psT2[0:64, :],
                                          qkb[:, 1024 + kv * 64:1024 + (kv + 1) * 64],
                                          trb[0:64, :])
                                    V(lambda e, kv=kv, stt=stt: e.tensor_copy(
                                        KD[kv][0:64, stt * P:(stt + 1) * P], trb[0:64, :]))
                                    DMA(lambda e, kv=kv, stt=stt: e.dma_start(
                                        KD[kv][64:128, stt * P:(stt + 1) * P],
                                        KD[kv][0:64, stt * P:(stt + 1) * P]))
                                V(lambda e, stt=stt: e.tensor_copy(
                                    V65[:, stt * 260:stt * 260 + 260]
                                    .rearrange("p (k e) -> p k e", e=65)[:, :, 0:64],
                                    qkb[:, 1280:1536].rearrange("p (k d) -> p k d", d=64)))
                        attn_core(r, QT, lambda h: (KD[h // 4], 0), V65, 260,
                                  Wo_sb, None, True, mlist)

        def l23_attn(layer):
            wqkv, bqkv = (w2_qkv, b2_qkv) if layer == 2 else (w3_qkv, b3_qkv)
            wo, bo = (w2_o, b2_o) if layer == 2 else (w3_o, b3_o)
            for r in range(2):
                with contextlib.ExitStack() as ls:
                    lb = lambda n, sh, dt=F32: ls.enter_context(
                        nc.sbuf_tensor(nm(n), sh, dt))
                    W_sb = lb("wA", [P, 8 * 3072], BF16)
                    stg = lb("stg2", [P, 512])
                    bq_sb = lb("bq", [1, 3072], BF16)
                    load_w_bf(W_sb, wqkv, 8, 3072, stg)
                    load_bias_bf(bq_sb, bqkv, 3072, stg)
                    if layer == 3:
                        Wl_sb = lb("wL", [P, 8 * D], BF16)
                        bl_sb = lb("bl", [1, D], BF16)
                        load_w_bf(Wl_sb, w3_l, 8, D, stg)
                        load_bias_bf(bl_sb, b3_l, D, stg)
                    xT = lb("xT2", [P, 8 * P], BF16)
                    qk = lb("qk2", [P, 3072])
                    qkb = lb("qkb2", [P, 3072], BF16)
                    trb = lb("trb4", [P, P], BF16)
                    if layer == 2:
                        rot = lb("rot2", [P, D])
                        cos_t = lb("cos2", [P, D]); sin_t = lb("sin2", [P, D])
                    else:
                        kvT = lb("kvT", [P, 16 * P], BF16)
                    psQ = ls.enter_context(nc.psum_tensor(nm("psQ2"), [P, 3072], F32))
                    psT2 = ls.enter_context(nc.psum_tensor(nm("psT3"), [P, P], F32))
                    for stt in range(16):
                        load_xT(xT, XNT, r * 2048 + stt * P)
                        dense(psQ, xT, W_sb, 3072, bias_sb=bq_sb)
                        V(lambda e: e.tensor_copy(qk[:], psQ[:, 0:3072]))
                        if layer == 2:
                            DMA(lambda e, stt=stt: e.dma_start(
                                cos_t[:], costab[stt * P:(stt + 1) * P, 0:D]))
                            DMA(lambda e, stt=stt: e.dma_start(
                                sin_t[:], sintab[stt * P:(stt + 1) * P, 0:D]))
                            rope(qk, rot, cos_t, sin_t, 0, D)
                            rope(qk, rot, cos_t, sin_t, D, D)
                        V(lambda e: e.tensor_copy(qkb[:], qk[:]))
                        for c in range(8):
                            tr128(psT2[:], qkb[:, c * P:(c + 1) * P], trb[:])
                            DMA(lambda e, c=c, stt=stt: e.dma_start(
                                QTH[c * P:(c + 1) * P, r * 2048 + stt * P:
                                    r * 2048 + (stt + 1) * P], trb[:]))
                        if layer == 2:
                            for c in range(8):
                                tr128(psT2[:], qkb[:, D + c * P:D + (c + 1) * P], trb[:])
                                DMA(lambda e, c=c, stt=stt: e.dma_start(
                                    KTH[c * P:(c + 1) * P, r * 2048 + stt * P:
                                        r * 2048 + (stt + 1) * P], trb[:]))
                            V(lambda e, stt=stt: e.tensor_copy(
                                qkb[:, 0:1040 + 0].rearrange(
                                    "p (k e) -> p k e", e=65)[:, :, 0:64],
                                qk[:, 2 * D:3 * D].rearrange("p (k d) -> p k d", d=64)))
                            V(lambda e: e.memset(
                                qkb[:, 0:1040].rearrange(
                                    "p (k e) -> p k e", e=65)[:, :, 64:65], 1.0))
                            DMA(lambda e, stt=stt, r=r: e.dma_start(
                                V65H[r * P:(r + 1) * P, stt * 1040:(stt + 1) * 1040],
                                qkb[:, 0:1040]))
                        else:
                            for c in range(16):
                                tr128(psT2[:], qkb[:, D + c * P:D + (c + 1) * P],
                                      kvT[:, c * P:(c + 1) * P])
                            dense(psQ, kvT, Wl_sb, D, bias_sb=bl_sb)
                            V(lambda e: e.tensor_copy(qkb[:, 0:D], psQ[:, 0:D]))
                            for c in range(8):
                                tr128(psT2[:], qkb[:, c * P:(c + 1) * P], trb[:])
                                DMA(lambda e, c=c, stt=stt: e.dma_start(
                                    KTH[c * P:(c + 1) * P, r * 2048 + stt * P:
                                        r * 2048 + (stt + 1) * P], trb[:]))
                            dense(psQ, kvT[:, 8 * P:16 * P], Wl_sb, D, bias_sb=bl_sb)
                            V(lambda e: e.tensor_copy(qk[:, 0:D], psQ[:, 0:D]))
                            V(lambda e, stt=stt: e.tensor_copy(
                                qkb[:, 0:1040].rearrange(
                                    "p (k e) -> p k e", e=65)[:, :, 0:64],
                                qk[:, 0:D].rearrange("p (k d) -> p k d", d=64)))
                            V(lambda e: e.memset(
                                qkb[:, 0:1040].rearrange(
                                    "p (k e) -> p k e", e=65)[:, :, 64:65], 1.0))
                            DMA(lambda e, stt=stt, r=r: e.dma_start(
                                V65H[r * P:(r + 1) * P, stt * 1040:(stt + 1) * 1040],
                                qkb[:, 0:1040]))
                with contextlib.ExitStack() as ls:
                    lb = lambda n, sh, dt=F32: ls.enter_context(
                        nc.sbuf_tensor(nm(n), sh, dt))
                    QT = lb("QT2", [P, 8 * 2048], BF16)
                    KT = lb("KT2", [P, 8 * 2048], BF16)
                    V65 = lb("V653", [P, 16 * 1040], BF16)
                    Wo_sb = lb("wO", [P, 8 * D], BF16)
                    stg = lb("stg7", [P, 512])
                    bo_sb = lb("bo2", [1, D], BF16)
                    load_w_bf(Wo_sb, wo, 8, D, stg)
                    load_bias_bf(bo_sb, bo, D, stg)
                    for c in range(8):
                        DMA(lambda e, c=c: e.dma_start(
                            QT[:, c * 2048:(c + 1) * 2048],
                            QTH[c * P:(c + 1) * P, r * 2048:(r + 1) * 2048]))
                        DMA(lambda e, c=c: e.dma_start(
                            KT[:, c * 2048:(c + 1) * 2048],
                            KTH[c * P:(c + 1) * P, r * 2048:(r + 1) * 2048]))
                    DMA(lambda e: e.dma_start(
                        V65[:], V65H[r * P:(r + 1) * P, :]))
                    attn_core(r, QT, lambda h: (KT, (h // 2) * 2048), V65, 1040,
                              Wo_sb, bo_sb, False, None)

        def l1_attn():
            with contextlib.ExitStack() as ls:
                lb = lambda n, sh, dt=F32: ls.enter_context(nc.sbuf_tensor(nm(n), sh, dt))
                W_sb = lb("w1i", [P, 8 * 3072], BF16)
                stg = lb("stg3", [P, 512])
                b_sb = lb("b1i", [1, 3072], BF16)
                load_w_bf(W_sb, w1_inp, 8, 3072, stg)
                load_bias_bf(b_sb, b1_inp, 3072, stg)
                xT = lb("xT3", [P, 8 * P], BF16)
                qvb = lb("qvb", [P, D], BF16)
                psQ = ls.enter_context(nc.psum_tensor(nm("psQ3"), [P, 3072], F32))
                for t in range(NT):
                    load_xT(xT, XNT, t * P)
                    dense(psQ, xT, W_sb, 3072, bias_sb=b_sb)
                    for j, dst in ((0, QL), (1, KL), (2, VL)):
                        V(lambda e, j=j: e.tensor_copy(
                            qvb[:].rearrange("p (h d) -> p h d", d=64),
                            psQ[:, 0:3072].rearrange("p (h t d) -> p h t d",
                                                     t=3, d=64)[:, :, j, :]))
                        DMA(lambda e, t=t, dst=dst: e.dma_start(
                            dst[t * P:(t + 1) * P, :], qvb[:]))
            GP(lambda e: e.collective_compute(
                "AllGather", ALU.bypass, replica_groups=[list(range(NCORE))],
                ins=[KL], outs=[KG]))
            GP(lambda e: e.collective_compute(
                "AllGather", ALU.bypass, replica_groups=[list(range(NCORE))],
                ins=[VL], outs=[VG]))
            with contextlib.ExitStack() as ls:
                lb = lambda n, sh, dt=F32: ls.enter_context(nc.sbuf_tensor(nm(n), sh, dt))
                Kt = [lb("Kt", [P, D]) for _ in range(16)]
                Vt = [lb("Vt", [P, D]) for _ in range(16)]
                kb = lb("kb", [P, D], BF16)
                Wo_sb = lb("w1o", [P, 8 * D], BF16)
                stg = lb("stg4", [P, 512])
                bo_sb = lb("b1o", [1, D], BF16)
                load_w_bf(Wo_sb, w1_outp, 8, D, stg)
                load_bias_bf(bo_sb, b1_outp, D, stg)
                qs_ = [lb("qs", [P, D]) for _ in range(2)]
                SC = lb("SC", [P, 512]); Ee = lb("Ee", [P, 512])
                prod = lb("prod", [P, D])
                Dn = lb("Dn", [P, 32])
                CTx = lb("CTx", [P, D]); tmp2 = lb("tmp2", [P, D])
                ctb2 = lb("ctb2", [P, D], BF16)
                ctT2 = lb("ctT2", [P, 8 * P], BF16)
                pjf = lb("pjf2", [P, D])
                psT4 = ls.enter_context(nc.psum_tensor(nm("psT4"), [P, P], F32))
                psJ2 = ls.enter_context(nc.psum_tensor(nm("psJ2"), [P, D], F32))
                for sc in range(16):
                    for s in range(2):
                        DMA(lambda e, s=s, sc=sc: e.dma_start(
                            kb[:], QL[s * 2048 + sc * P:s * 2048 + (sc + 1) * P, :]))
                        V(lambda e, s=s: e.tensor_copy(qs_[s][:], kb[:]))
                    for tt in range(16):
                        DMA(lambda e, tt=tt, sc=sc: e.dma_start(
                            kb[:], KG[tt * 2048 + sc * P:tt * 2048 + (sc + 1) * P, :]))
                        V(lambda e, tt=tt: e.tensor_copy(Kt[tt][:], kb[:]))
                        DMA(lambda e, tt=tt, sc=sc: e.dma_start(
                            kb[:], VG[tt * 2048 + sc * P:tt * 2048 + (sc + 1) * P, :]))
                        V(lambda e, tt=tt: e.tensor_copy(Vt[tt][:], kb[:]))
                    for s in range(2):
                        for tt in range(16):
                            V(lambda e, s=s, tt=tt: e.tensor_tensor(
                                prod[:], qs_[s][:], Kt[tt][:], ALU.mult))
                            V(lambda e, s=s, tt=tt: e.tensor_reduce(
                                SC[:].rearrange("p (s2 h t) -> p s2 h t", s2=2, h=16)
                                [:, s, :, tt],
                                prod[:].rearrange("p (g d) -> p g d", d=64),
                                AX.X, ALU.add))
                    S(lambda e: e.activation(Ee[:], SC[:], ACT.Exp, scale=0.125))
                    V(lambda e: e.tensor_reduce(
                        Dn[:], Ee[:].rearrange("p (g t) -> p g t", t=16), AX.X, ALU.add))
                    V(lambda e: e.reciprocal(Dn[:], Dn[:]))
                    for s in range(2):
                        for tt in range(16):
                            tgt = CTx if tt == 0 else tmp2
                            V(lambda e, s=s, tt=tt, tgt=tgt: e.tensor_tensor(
                                tgt[:].rearrange("p (h d) -> p h d", d=64),
                                Vt[tt][:].rearrange("p (h d) -> p h d", d=64),
                                Ee[:].rearrange("p (s2 h t) -> p s2 h t", s2=2, h=16)
                                [:, s, :, tt:tt + 1].broadcast_to([P, 16, 64]),
                                ALU.mult))
                            if tt > 0:
                                V(lambda e: e.tensor_tensor(CTx[:], CTx[:], tmp2[:],
                                                            ALU.add))
                        V(lambda e, s=s: e.tensor_tensor(
                            CTx[:].rearrange("p (h d) -> p h d", d=64),
                            CTx[:].rearrange("p (h d) -> p h d", d=64),
                            Dn[:].rearrange("p (s2 h) -> p s2 h", s2=2)[:, s, :]
                            .rearrange("p (h o) -> p h o", o=1)
                            .broadcast_to([P, 16, 64]), ALU.mult))
                        V(lambda e: e.tensor_copy(ctb2[:], CTx[:]))
                        for c in range(8):
                            tr128(psT4[:], ctb2[:, c * P:(c + 1) * P],
                                  ctT2[:, c * P:(c + 1) * P])
                        dense(psJ2, ctT2, Wo_sb, D, bias_sb=bo_sb)
                        V(lambda e: e.tensor_copy(pjf[:], psJ2[:, 0:D]))
                        tg = s * 16 + sc
                        DMA(lambda e, tg=tg: e.dma_start(
                            AOUT[tg * P:(tg + 1) * P, :], pjf[:]))

        def l0_ffn():
            with contextlib.ExitStack() as ls:
                lb = lambda n, sh, dt=F32: ls.enter_context(nc.sbuf_tensor(nm(n), sh, dt))
                W1 = lb("W1", [P, 8 * 4096], BF16)
                stg = lb("stg5", [P, 512])
                b1s = lb("b1s", [P, 32])
                load_w_bf(W1, w0_f1, 8, 4096, stg)
                for wc in range(32):
                    DMA(lambda e, wc=wc: e.dma_start(
                        b1s[:, wc:wc + 1], b0_f1[wc:wc + 1, :].rearrange("o p -> p o"),
                        allow_slow_non_contiguous=True))
                xTn = lb("xTn", [P, 8 * 512], BF16)
                hb = lb("hb", [P, 512], BF16)
                psH = ls.enter_context(nc.psum_tensor(nm("psH"), [P, 512], F32))
                for tb in range(8):
                    DMA(lambda e, tb=tb: e.dma_start(
                        xTn[:, 0:8 * 512].rearrange("p (k n) -> p k n", n=512),
                        XNT[:, tb * 512:(tb + 1) * 512]
                        .rearrange("(k p) n -> p k n", p=P)))
                    for wc in range(32):
                        for kc in range(8):
                            TE(lambda e, wc=wc, kc=kc: e.matmul(
                                psH[:, 0:512],
                                W1[:, kc * 4096 + wc * P:kc * 4096 + (wc + 1) * P],
                                xTn[:, kc * 512:(kc + 1) * 512],
                                start=(kc == 0), stop=(kc == 7),
                                skip_group_check=True))
                        S(lambda e, wc=wc: e.activation(
                            hb[:], psH[:, 0:512], ACT.Relu, bias=b1s[:, wc:wc + 1],
                            scale=1.0))
                        DMA(lambda e, wc=wc, tb=tb: e.dma_start(
                            HT[wc * P:(wc + 1) * P, tb * 512:(tb + 1) * 512], hb[:]))
            with contextlib.ExitStack() as ls:
                lb = lambda n, sh, dt=F32: ls.enter_context(nc.sbuf_tensor(nm(n), sh, dt))
                W2 = lb("W2", [P, 32 * D], BF16)
                stg = lb("stg6", [P, 512])
                load_w_bf(W2, w0_f2, 32, D, stg)
                b2s = lb("b2s", [1, D], BF16)
                load_bias_bf(b2s, b0_f2, D, stg)
                hT = lb("hT", [P, 32 * P], BF16)
                of = lb("of", [P, D])
                psO = ls.enter_context(nc.psum_tensor(nm("psO"), [P, D], F32))
                for t in range(NT):
                    DMA(lambda e, t=t: e.dma_start(
                        hT[:, 0:32 * P].rearrange("p (k n) -> p k n", n=P),
                        HT[:, t * P:(t + 1) * P].rearrange("(k p) n -> p k n", p=P)))
                    dense(psO, hT, W2, D, bias_sb=b2s, kchunks=32)
                    V(lambda e: e.tensor_copy(of[:], psO[:, 0:D]))
                    DMA(lambda e, t=t: e.dma_start(FOUT[t * P:(t + 1) * P, :], of[:]))

        def mk_qbufs(ls, W):
            lb = lambda n, sh, dt=F32: ls.enter_context(nc.sbuf_tensor(nm(n), sh, dt))
            return {"gm": lb("gm", [P, W // 16]), "gi": lb("gi", [P, W // 16]),
                    "tm": lb("tm", [P, W // 2]), "A": lb("A", [P, W]),
                    "B": lb("B", [P, W]), "C": lb("C", [P, W]),
                    "D2": lb("D2", [P, W]), "E2": lb("E2", [P, W])}

        def blockmax_scale(x_ap, W, qdiv, lg, q):
            V(lambda e: e.tensor_tensor(q["E2"][:, 0:W], x_ap, x_ap, ALU.mult))
            cur = W
            for _ in range(lg):
                half = cur // 2
                V(lambda e, cur=cur, half=half: e.tensor_tensor(
                    q["tm"][:, 0:half],
                    q["E2"][:, 0:cur].rearrange("p (g two) -> p g two", two=2)[:, :, 0],
                    q["E2"][:, 0:cur].rearrange("p (g two) -> p g two", two=2)[:, :, 1],
                    ALU.max))
                V(lambda e, half=half: e.tensor_copy(q["E2"][:, 0:half],
                                                     q["tm"][:, 0:half]))
                cur = half
            S(lambda e, cur=cur: e.activation(q["gm"][:, 0:cur], q["E2"][:, 0:cur],
                                              ACT.Sqrt, scale=1.0))
            V(lambda e, cur=cur: e.tensor_scalar(
                q["gm"][:, 0:cur], q["gm"][:, 0:cur], 1.0 / qdiv, 1e-12,
                ALU.mult, ALU.max))
            V(lambda e, cur=cur: e.reciprocal(q["gi"][:, 0:cur], q["gm"][:, 0:cur]))

        def bc(ap, G, d):
            return ap.rearrange("p (g o) -> p g o", o=1).broadcast_to([P, G, d])

        def fp4_qdq(x_ap, out_bf, W, q):
            G = W // 16
            blockmax_scale(x_ap, W, 6.0, 4, q)
            S(lambda e: e.activation(q["A"][:, 0:W], x_ap, ACT.Abs, scale=1.0))
            V(lambda e: e.tensor_tensor(
                q["A"][:, 0:W].rearrange("p (g d) -> p g d", d=16),
                q["A"][:, 0:W].rearrange("p (g d) -> p g d", d=16),
                bc(q["gi"][:, 0:G], G, 16), ALU.mult))
            V(lambda e: e.tensor_scalar(q["B"][:, 0:W], q["A"][:, 0:W], 2.0, MAGIC,
                                        ALU.mult, ALU.add))
            V(lambda e: e.tensor_scalar(q["B"][:, 0:W], q["B"][:, 0:W], -MAGIC, 0.5,
                                        ALU.add, ALU.mult))
            V(lambda e: e.tensor_scalar(q["B"][:, 0:W], q["B"][:, 0:W], 2.0, None,
                                        ALU.min))
            V(lambda e: e.tensor_scalar(q["C"][:, 0:W], q["A"][:, 0:W], MAGIC, -MAGIC,
                                        ALU.add, ALU.add))
            V(lambda e: e.tensor_scalar(q["C"][:, 0:W], q["C"][:, 0:W], 2.0, 4.0,
                                        ALU.max, ALU.min))
            V(lambda e: e.tensor_scalar(q["D2"][:, 0:W], q["A"][:, 0:W], 5.0, 2.0,
                                        ALU.is_gt, ALU.mult))
            V(lambda e: e.tensor_tensor(q["B"][:, 0:W], q["B"][:, 0:W], q["C"][:, 0:W],
                                        ALU.add))
            V(lambda e: e.tensor_scalar(q["D2"][:, 0:W], q["D2"][:, 0:W], -2.0, None,
                                        ALU.add))
            V(lambda e: e.tensor_tensor(q["B"][:, 0:W], q["B"][:, 0:W], q["D2"][:, 0:W],
                                        ALU.add))
            S(lambda e: e.activation(q["E2"][:, 0:W], x_ap, ACT.Sign, scale=1.0))
            V(lambda e: e.tensor_tensor(
                q["E2"][:, 0:W].rearrange("p (g d) -> p g d", d=16),
                q["E2"][:, 0:W].rearrange("p (g d) -> p g d", d=16),
                bc(q["gm"][:, 0:G], G, 16), ALU.mult))
            V(lambda e: e.tensor_tensor(q["B"][:, 0:W], q["B"][:, 0:W], q["E2"][:, 0:W],
                                        ALU.mult))
            V(lambda e: e.tensor_copy(out_bf[:, 0:W], q["B"][:, 0:W]))

        def lpbq_qdq(x_ap, out_bf, W, q):
            G = W // 32
            blockmax_scale(x_ap, W, 7.0, 5, q)
            V(lambda e: e.tensor_tensor(
                q["A"][:, 0:W].rearrange("p (g d) -> p g d", d=32),
                x_ap.rearrange("p (g d) -> p g d", d=32),
                bc(q["gi"][:, 0:G], G, 32), ALU.mult))
            V(lambda e: e.tensor_scalar(q["A"][:, 0:W], q["A"][:, 0:W], MAGIC, -MAGIC,
                                        ALU.add, ALU.add))
            V(lambda e: e.tensor_tensor(
                q["A"][:, 0:W].rearrange("p (g d) -> p g d", d=32),
                q["A"][:, 0:W].rearrange("p (g d) -> p g d", d=32),
                bc(q["gm"][:, 0:G], G, 32), ALU.mult))
            V(lambda e: e.tensor_copy(out_bf[:, 0:W], q["A"][:, 0:W]))

        def l1_ffn():
            with contextlib.ExitStack() as ls:
                lb = lambda n, sh, dt=F32: ls.enter_context(nc.sbuf_tensor(nm(n), sh, dt))
                q = mk_qbufs(ls, D)
                xt = lb("xt4", [P, D]); xqb = lb("xqb", [P, D], BF16)
                trb = lb("trb6", [P, P], BF16)
                psT5 = ls.enter_context(nc.psum_tensor(nm("psT5"), [P, P], F32))
                for t in range(NT):
                    DMA(lambda e, t=t: e.dma_start(xt[:], XN[t * P:(t + 1) * P, :]))
                    fp4_qdq(xt[:], xqb, D, q)
                    for c in range(8):
                        tr128(psT5[:], xqb[:, c * P:(c + 1) * P], trb[:])
                        DMA(lambda e, t=t, c=c: e.dma_start(
                            XQT[c * P:(c + 1) * P, t * P:(t + 1) * P], trb[:]))
            with contextlib.ExitStack() as ls:
                lb = lambda n, sh, dt=F32: ls.enter_context(nc.sbuf_tensor(nm(n), sh, dt))
                WGT = lb("WGT", [P, 8 * 2816], BF16)
                WUT = lb("WUT", [P, 8 * 2816], BF16)
                q = mk_qbufs(ls, 2816)
                wstg = lb("wstg", [P, D]); wqb = lb("wqb", [P, 2816], BF16)
                trb = lb("trb7", [P, P], BF16)
                psT6 = ls.enter_context(nc.psum_tensor(nm("psT6"), [P, P], F32))
                for wsrc, Wdst in ((w1_gate, WGT), (w1_up, WUT)):
                    for oc in range(22):
                        DMA(lambda e, oc=oc, wsrc=wsrc: e.dma_start(
                            wstg[:], wsrc[oc * P:(oc + 1) * P, :]))
                        fp4_qdq(wstg[:], wqb, D, q)
                        for ic in range(8):
                            tr128(psT6[:], wqb[:, ic * P:(ic + 1) * P], trb[:])
                            V(lambda e, ic=ic, oc=oc, Wdst=Wdst: e.tensor_copy(
                                Wdst[:, ic * 2816 + oc * P:ic * 2816 + (oc + 1) * P],
                                trb[:]))
                xT = lb("xT5", [P, 8 * P], BF16)
                Hf = lb("Hf", [P, 2816])
                gsb = lb("gsb", [P, 512])
                hqb = lb("hqb", [P, 2816], BF16)
                psG = ls.enter_context(nc.psum_tensor(nm("psG"), [P, 512], F32))
                psU = ls.enter_context(nc.psum_tensor(nm("psU"), [P, 512], F32))
                for t in range(NT):
                    load_xT(xT, XQT, t * P)
                    for b in range(6):
                        w = min(512, 2816 - b * 512)
                        for kc in range(8):
                            TE(lambda e, b=b, w=w, kc=kc: e.matmul(
                                psG[:, 0:w], xT[:, kc * P:(kc + 1) * P],
                                WGT[:, kc * 2816 + b * 512:kc * 2816 + b * 512 + w],
                                start=(kc == 0), stop=(kc == 7), skip_group_check=True))
                        for kc in range(8):
                            TE(lambda e, b=b, w=w, kc=kc: e.matmul(
                                psU[:, 0:w], xT[:, kc * P:(kc + 1) * P],
                                WUT[:, kc * 2816 + b * 512:kc * 2816 + b * 512 + w],
                                start=(kc == 0), stop=(kc == 7), skip_group_check=True))
                        S(lambda e, w=w: e.activation(gsb[:, 0:w], psG[:, 0:w],
                                                      ACT.Silu, scale=1.0))
                        V(lambda e, b=b, w=w: e.tensor_tensor(
                            Hf[:, b * 512:b * 512 + w], gsb[:, 0:w], psU[:, 0:w],
                            ALU.mult))
                    fp4_qdq(Hf[:], hqb, 2816, q)
                    for c in range(22):
                        tr128(psT6[:], hqb[:, c * P:(c + 1) * P], trb[:])
                        DMA(lambda e, t=t, c=c: e.dma_start(
                            HQT[c * P:(c + 1) * P, t * P:(t + 1) * P], trb[:]))
            with contextlib.ExitStack() as ls:
                lb = lambda n, sh, dt=F32: ls.enter_context(nc.sbuf_tensor(nm(n), sh, dt))
                WDT = lb("WDT", [P, 22 * D], BF16)
                q = mk_qbufs(ls, 2816)
                wstg = lb("wstg2", [P, 2816]); wqb = lb("wqb2", [P, 2816], BF16)
                trb = lb("trb8", [P, P], BF16)
                psT7 = ls.enter_context(nc.psum_tensor(nm("psT7"), [P, P], F32))
                for oc in range(8):
                    DMA(lambda e, oc=oc: e.dma_start(
                        wstg[:], w1_down[oc * P:(oc + 1) * P, :]))
                    fp4_qdq(wstg[:], wqb, 2816, q)
                    for ic in range(22):
                        tr128(psT7[:], wqb[:, ic * P:(ic + 1) * P], trb[:])
                        V(lambda e, ic=ic, oc=oc: e.tensor_copy(
                            WDT[:, ic * D + oc * P:ic * D + (oc + 1) * P], trb[:]))
                hT = lb("hT2", [P, 22 * P], BF16)
                of = lb("of2", [P, D])
                psO2 = ls.enter_context(nc.psum_tensor(nm("psO2"), [P, D], F32))
                for t in range(NT):
                    DMA(lambda e, t=t: e.dma_start(
                        hT[:, 0:22 * P].rearrange("p (k n) -> p k n", n=P),
                        HQT[:, t * P:(t + 1) * P].rearrange("(k p) n -> p k n", p=P)))
                    dense(psO2, hT, WDT, D, kchunks=22)
                    V(lambda e: e.tensor_copy(of[:], psO2[:, 0:D]))
                    DMA(lambda e, t=t: e.dma_start(FOUT[t * P:(t + 1) * P, :], of[:]))

        def l2_ffn():
            with contextlib.ExitStack() as ls:
                lb = lambda n, sh, dt=F32: ls.enter_context(nc.sbuf_tensor(nm(n), sh, dt))
                WTT = lb("WTT", [P, 8 * D], BF16)
                pk = lb("pk", [P, D], I32)
                tm_i = lb("tm_i", [P, D], I32)
                tm_f = lb("tm_f", [P, D])
                wb = lb("wb2", [P, D], BF16)
                trb = lb("trb9", [P, P], BF16)
                psT8 = ls.enter_context(nc.psum_tensor(nm("psT8"), [P, P], F32))
                for half in range(2):
                    DMA(lambda e, half=half: e.dma_start(
                        pk[:], w2_pk[half * P:(half + 1) * P, :]))
                    for j in range(4):
                        oc = j * 2 + half
                        V(lambda e, j=j: e.tensor_scalar(
                            tm_i[:], pk[:], 2 * j, 3, ALU.logical_shift_right,
                            ALU.bitwise_and))
                        V(lambda e: e.tensor_copy(tm_f[:], tm_i[:]))
                        V(lambda e: e.tensor_scalar(tm_f[:], tm_f[:], -1.0, None,
                                                    ALU.add))
                        V(lambda e: e.tensor_copy(wb[:], tm_f[:]))
                        for ic in range(8):
                            tr128(psT8[:], wb[:, ic * P:(ic + 1) * P], trb[:])
                            V(lambda e, ic=ic, oc=oc: e.tensor_copy(
                                WTT[:, ic * D + oc * P:ic * D + (oc + 1) * P], trb[:]))
                scr = lb("scr", [P, 1])
                DMA(lambda e: e.dma_start(scr[:], w2_scale.broadcast_to([P, 1])))
                brep = lb("brep", [P, D])
                DMA(lambda e: e.dma_start(brep[:], b2_f.broadcast_to([P, D])))
                xt = lb("xt5", [P, D]); mx = lb("mx", [P, 1]); iv2 = lb("iv2", [P, 1])
                scomb = lb("scomb", [P, 1])
                xr = lb("xr", [P, D]); xrb = lb("xrb", [P, D], BF16)
                xqT = lb("xqT", [P, 8 * P], BF16)
                of = lb("of3", [P, D])
                psO3 = ls.enter_context(nc.psum_tensor(nm("psO3"), [P, D], F32))
                for t in range(NT):
                    DMA(lambda e, t=t: e.dma_start(xt[:], XN[t * P:(t + 1) * P, :]))
                    V(lambda e: e.tensor_reduce(mx[:], xt[:], AX.X, ALU.max,
                                                apply_absolute_value=True))
                    V(lambda e: e.tensor_scalar(mx[:], mx[:], 1e-5, None, ALU.max))
                    V(lambda e: e.reciprocal(iv2[:], mx[:]))
                    V(lambda e: e.tensor_scalar(xr[:], xt[:], iv2[:], 127.0,
                                                ALU.mult, ALU.mult))
                    V(lambda e: e.tensor_scalar(xr[:], xr[:], MAGIC, -MAGIC,
                                                ALU.add, ALU.add))
                    V(lambda e: e.tensor_copy(xrb[:], xr[:]))
                    for c in range(8):
                        tr128(psT8[:], xrb[:, c * P:(c + 1) * P],
                              xqT[:, c * P:(c + 1) * P])
                    dense(psO3, xqT, WTT, D)
                    V(lambda e: e.tensor_tensor(scomb[:], mx[:], scr[:], ALU.mult))
                    V(lambda e: e.tensor_scalar(scomb[:], scomb[:], 1.0 / 127.0, None,
                                                ALU.mult))
                    V(lambda e: e.tensor_scalar(of[:], psO3[:, 0:D], scomb[:], None,
                                                ALU.mult))
                    V(lambda e: e.tensor_tensor(of[:], of[:], brep[:], ALU.add))
                    DMA(lambda e, t=t: e.dma_start(FOUT[t * P:(t + 1) * P, :], of[:]))

        def gmax_allreduce(gacc, idx, grep, g1):
            DMA(lambda e: e.dma_start(GD, gacc[:]))
            DMA(lambda e: e.dma_start(g1[0:1, 0:P], GD.rearrange("p o -> o p"),
                                      allow_slow_non_contiguous=True))
            V(lambda e: e.tensor_reduce(g1[0:1, P:P + 1], g1[0:1, 0:P], AX.X, ALU.max))
            DMA(lambda e, idx=idx: e.dma_start(ARI[idx], g1[0:1, P:P + 1]))
            GP(lambda e, idx=idx: e.collective_compute(
                "AllReduce", ALU.max, replica_groups=[list(range(NCORE))],
                ins=[ARI[idx]], outs=[ARO[idx]]))
            DMA(lambda e, idx=idx: e.dma_start(grep[:], ARO[idx].broadcast_to([P, 1])))
            V(lambda e: e.tensor_scalar(grep[:], grep[:], 1.0 / 32767.0, 1e-12,
                                        ALU.mult, ALU.max))

        def aqdq(dst, src_ap, sinv, sc, tmp):
            V(lambda e: e.tensor_scalar(tmp, src_ap, sinv[:], None, ALU.mult))
            V(lambda e: e.tensor_scalar(tmp, tmp, MAGIC, -MAGIC, ALU.add, ALU.add))
            V(lambda e: e.tensor_scalar(dst, tmp, sc[:], None, ALU.mult))

        def l3_ffn():
            with contextlib.ExitStack() as ls:
                lb = lambda n, sh, dt=F32: ls.enter_context(nc.sbuf_tensor(nm(n), sh, dt))
                gacc = lb("gacc", [P, 1]); tmax = lb("tmax", [P, 1])
                g1 = lb("g1", [1, P + 1])
                grx = lb("grx", [P, 1]); gix = lb("gix", [P, 1])
                gru = lb("gru", [P, 1]); giu = lb("giu", [P, 1])
                grg = lb("grg", [P, 1]); gig = lb("gig", [P, 1])
                gr2 = lb("gr2", [P, 1]); gi2 = lb("gi2", [P, 1])
                gro = lb("gro", [P, 1])
                xt = lb("xt6", [P, 3072]); tmpq = lb("tmpq", [P, 3072])
                xqb = lb("xqb2", [P, D], BF16)
                trb = lb("trbA", [P, P], BF16)
                psT9 = ls.enter_context(nc.psum_tensor(nm("psT9"), [P, P], F32))
                V(lambda e: e.memset(gacc[:], 0.0))
                for t in range(NT):
                    DMA(lambda e, t=t: e.dma_start(xt[:, 0:D], XN[t * P:(t + 1) * P, :]))
                    V(lambda e: e.tensor_reduce(tmax[:], xt[:, 0:D], AX.X, ALU.max,
                                                apply_absolute_value=True))
                    V(lambda e: e.tensor_tensor(gacc[:], gacc[:], tmax[:], ALU.max))
                gmax_allreduce(gacc, 0, grx, g1)
                V(lambda e: e.reciprocal(gix[:], grx[:]))
                WUT3 = lb("WUT3", [P, 8 * 3072], BF16)
                WGT3 = lb("WGT3", [P, 8 * 3072], BF16)
                q = mk_qbufs(ls, D)
                wstg = lb("wstg3", [P, D]); wqb = lb("wqb3", [P, D], BF16)
                for wsrc, Wdst in ((w3_up, WUT3), (w3_gate, WGT3)):
                    for oc in range(24):
                        DMA(lambda e, oc=oc, wsrc=wsrc: e.dma_start(
                            wstg[:], wsrc[oc * P:(oc + 1) * P, :]))
                        lpbq_qdq(wstg[:], wqb, D, q)
                        for ic in range(8):
                            tr128(psT9[:], wqb[:, ic * P:(ic + 1) * P], trb[:])
                            V(lambda e, ic=ic, oc=oc, Wdst=Wdst: e.tensor_copy(
                                Wdst[:, ic * 3072 + oc * P:ic * 3072 + (oc + 1) * P],
                                trb[:]))
                xT = lb("xT6", [P, 8 * P], BF16)
                upf = lb("upf", [P, 3072])
                gaccu = lb("gaccu", [P, 1]); gaccg = lb("gaccg", [P, 1])
                V(lambda e: e.memset(gaccu[:], 0.0))
                V(lambda e: e.memset(gaccg[:], 0.0))
                psM = ls.enter_context(nc.psum_tensor(nm("psM"), [P, 512], F32))
                for t in range(NT):
                    DMA(lambda e, t=t: e.dma_start(xt[:, 0:D], XN[t * P:(t + 1) * P, :]))
                    aqdq(xt[:, 0:D], xt[:, 0:D], gix, grx, tmpq[:, 0:D])
                    V(lambda e: e.tensor_copy(xqb[:], xt[:, 0:D]))
                    for c in range(8):
                        tr128(psT9[:], xqb[:, c * P:(c + 1) * P],
                              xT[:, c * P:(c + 1) * P])
                    for Wd, dstH, ga in ((WUT3, UPH, gaccu), (WGT3, GTH, gaccg)):
                        for b in range(6):
                            for kc in range(8):
                                TE(lambda e, b=b, kc=kc, Wd=Wd: e.matmul(
                                    psM[:, 0:512], xT[:, kc * P:(kc + 1) * P],
                                    Wd[:, kc * 3072 + b * 512:kc * 3072 + (b + 1) * 512],
                                    start=(kc == 0), stop=(kc == 7),
                                    skip_group_check=True))
                            V(lambda e, b=b: e.tensor_copy(
                                upf[:, b * 512:(b + 1) * 512], psM[:, 0:512]))
                        V(lambda e: e.tensor_reduce(tmax[:], upf[:], AX.X, ALU.max,
                                                    apply_absolute_value=True))
                        V(lambda e, ga=ga: e.tensor_tensor(ga[:], ga[:], tmax[:],
                                                           ALU.max))
                        DMA(lambda e, t=t, dstH=dstH: e.dma_start(
                            dstH[t * P:(t + 1) * P, :], upf[:]))
                gmax_allreduce(gaccu, 1, gru, g1)
                V(lambda e: e.reciprocal(giu[:], gru[:]))
                gmax_allreduce(gaccg, 2, grg, g1)
                V(lambda e: e.reciprocal(gig[:], grg[:]))
                sgb = lb("sgb", [P, 3072])
                V(lambda e: e.memset(gacc[:], 0.0))
                for t in range(NT):
                    DMA(lambda e, t=t: e.dma_start(xt[:], GTH[t * P:(t + 1) * P, :]))
                    aqdq(xt[:], xt[:], gig, grg, tmpq[:])
                    S(lambda e: e.activation(sgb[:], xt[:], ACT.Sigmoid, scale=1.0))
                    V(lambda e: e.tensor_scalar(sgb[:], sgb[:], 65536.0, MAGIC,
                                                ALU.mult, ALU.add))
                    V(lambda e: e.tensor_scalar(sgb[:], sgb[:], -MAGIC, 65535.0,
                                                ALU.add, ALU.min))
                    V(lambda e: e.tensor_scalar(sgb[:], sgb[:], 1.0 / 65536.0, None,
                                                ALU.mult))
                    V(lambda e: e.tensor_tensor(xt[:], xt[:], sgb[:], ALU.mult))
                    V(lambda e: e.tensor_reduce(tmax[:], xt[:], AX.X, ALU.max,
                                                apply_absolute_value=True))
                    V(lambda e: e.tensor_tensor(gacc[:], gacc[:], tmax[:], ALU.max))
                    DMA(lambda e, t=t: e.dma_start(G2H[t * P:(t + 1) * P, :], xt[:]))
                    DMA(lambda e, t=t: e.dma_start(xt[:], UPH[t * P:(t + 1) * P, :]))
                    aqdq(xt[:], xt[:], giu, gru, tmpq[:])
                    DMA(lambda e, t=t: e.dma_start(UQH[t * P:(t + 1) * P, :], xt[:]))
                gmax_allreduce(gacc, 3, gr2, g1)
                V(lambda e: e.reciprocal(gi2[:], gr2[:]))
                V(lambda e: e.memset(gaccu[:], 0.0))
                for t in range(NT):
                    DMA(lambda e, t=t: e.dma_start(xt[:], G2H[t * P:(t + 1) * P, :]))
                    aqdq(xt[:], xt[:], gi2, gr2, tmpq[:])
                    DMA(lambda e, t=t: e.dma_start(tmpq[:], UQH[t * P:(t + 1) * P, :]))
                    V(lambda e: e.tensor_tensor(xt[:], xt[:], tmpq[:], ALU.mult))
                    V(lambda e: e.tensor_reduce(tmax[:], xt[:], AX.X, ALU.max,
                                                apply_absolute_value=True))
                    V(lambda e: e.tensor_tensor(gaccu[:], gaccu[:], tmax[:], ALU.max))
                    DMA(lambda e, t=t: e.dma_start(OH[t * P:(t + 1) * P, :], xt[:]))
                gmax_allreduce(gaccu, 4, gro, g1)
            with contextlib.ExitStack() as ls:
                lb = lambda n, sh, dt=F32: ls.enter_context(nc.sbuf_tensor(nm(n), sh, dt))
                WDT3 = lb("WDT3", [P, 24 * D], BF16)
                q = mk_qbufs(ls, 3072)
                wstg = lb("wstg4", [P, 3072]); wqb = lb("wqb4", [P, 3072], BF16)
                trb = lb("trbB", [P, P], BF16)
                gro2 = lb("gro2", [P, 1]); gio2 = lb("gio2", [P, 1])
                DMA(lambda e: e.dma_start(gro2[:], ARO[4].broadcast_to([P, 1])))
                V(lambda e: e.tensor_scalar(gro2[:], gro2[:], 1.0 / 32767.0, 1e-12,
                                            ALU.mult, ALU.max))
                V(lambda e: e.reciprocal(gio2[:], gro2[:]))
                psTA = ls.enter_context(nc.psum_tensor(nm("psTA"), [P, P], F32))
                for oc in range(8):
                    DMA(lambda e, oc=oc: e.dma_start(
                        wstg[:], w3_down[oc * P:(oc + 1) * P, :]))
                    lpbq_qdq(wstg[:], wqb, 3072, q)
                    for ic in range(24):
                        tr128(psTA[:], wqb[:, ic * P:(ic + 1) * P], trb[:])
                        V(lambda e, ic=ic, oc=oc: e.tensor_copy(
                            WDT3[:, ic * D + oc * P:ic * D + (oc + 1) * P], trb[:]))
                xt = lb("xt7", [P, 3072]); tmpq = lb("tmpq2", [P, 3072])
                oqb = lb("oqb", [P, 3072], BF16)
                oT = lb("oT", [P, 24 * P], BF16)
                of = lb("of4", [P, D])
                psO4 = ls.enter_context(nc.psum_tensor(nm("psO4"), [P, D], F32))
                for t in range(NT):
                    DMA(lambda e, t=t: e.dma_start(xt[:], OH[t * P:(t + 1) * P, :]))
                    aqdq(xt[:], xt[:], gio2, gro2, tmpq[:])
                    V(lambda e: e.tensor_copy(oqb[:], xt[:]))
                    for c in range(24):
                        tr128(psTA[:], oqb[:, c * P:(c + 1) * P],
                              oT[:, c * P:(c + 1) * P])
                    dense(psO4, oT, WDT3, D, kchunks=24)
                    V(lambda e: e.tensor_copy(of[:], psO4[:, 0:D]))
                    DMA(lambda e, t=t: e.dma_start(FOUT[t * P:(t + 1) * P, :], of[:]))

        def update(final, first=False):
            with contextlib.ExitStack() as ls:
                lb = lambda n, sh, dt=F32: ls.enter_context(nc.sbuf_tensor(nm(n), sh, dt))
                a = lb("ua", [P, D]); b = lb("ub", [P, D]); c = lb("uc", [P, D])
                srcr = x_in if first else X
                for t in range(NT):
                    DMA(lambda e, t=t, srcr=srcr: e.dma_start(
                        a[:], srcr[t * P:(t + 1) * P, :]))
                    DMA(lambda e, t=t: e.dma_start(b[:], AOUT[t * P:(t + 1) * P, :]))
                    DMA(lambda e, t=t: e.dma_start(c[:], FOUT[t * P:(t + 1) * P, :]))
                    V(lambda e: e.tensor_tensor(a[:], a[:], b[:], ALU.add))
                    V(lambda e: e.tensor_tensor(a[:], a[:], c[:], ALU.add))
                    dst = out_ext if final else X
                    DMA(lambda e, t=t, dst=dst: e.dma_start(
                        dst[t * P:(t + 1) * P, :], a[:]))

        def ckpt(src):
            with contextlib.ExitStack() as ls:
                buf = ls.enter_context(nc.sbuf_tensor(nm("ck"), [P, D], F32))
                for t in range(NT):
                    DMA(lambda e, t=t: e.dma_start(buf[:], src[t * P:(t + 1) * P, :]))
                    DMA(lambda e, t=t: e.dma_start(
                        out_ext[t * P:(t + 1) * P, :], buf[:]))

        stages = [
            ("ln0", lambda: ln_stage(x_in), XN), ("a0", l0_attn, AOUT), ("f0", l0_ffn, FOUT),
            ("u0", lambda: update(False, True), X),
            ("ln1", ln_stage, XN), ("a1", l1_attn, AOUT), ("f1", l1_ffn, FOUT),
            ("u1", lambda: update(False), X),
            ("ln2", ln_stage, XN), ("a2", lambda: l23_attn(2), AOUT),
            ("f2", l2_ffn, FOUT), ("u2", lambda: update(False), X),
            ("ln3", ln_stage, XN), ("a3", lambda: l23_attn(3), AOUT),
            ("f3", l3_ffn, FOUT), ("u3", lambda: update(True), None),
        ]
        for name, fn, dump in stages:
            fn()
            if dbg is not None and name == dbg:
                if dump is not None:
                    ckpt(dump)
                break

        if st8["pend"] is not None:
            st8["pend"].then_inc(sem, 1)
            st8["cnt"] += 1
            st8["pend"] = None
        nc.sync.wait_ge(sem, st8["cnt"])
    return nc


_NC = None


def kernel(**inputs):
    global _NC
    f = lambda k: np.ascontiguousarray(np.asarray(inputs[k], np.float32))
    x = f("L0_attn_x")
    t = np.arange(2048, dtype=np.float64)
    invf = 1.0 / (10000.0 ** (np.arange(0, 64, 2, dtype=np.float64) / 64))
    emb = np.repeat(t[:, None] * invf[None, :], 2, axis=-1)
    cos1 = np.cos(emb).astype(np.float32)
    sin1 = np.sin(emb).astype(np.float32)
    costab = np.tile(cos1, (1, 20)).astype(np.float32)
    sintab = np.tile(sin1, (1, 20)).astype(np.float32)
    masks = np.concatenate([
        np.where(np.arange(512)[None, :] >= (j * P + np.arange(P))[:, None],
                 0.0, NEG).astype(np.float32)
        for j in range(4)], axis=0)
    common = {
        "w0_attn": f("L0_attn_w_attn"), "w0_proj": f("L0_attn_w_proj"),
        "w0_f1": f("L0_ffn_w1"), "b0_f1": f("L0_ffn_b1").reshape(32, 128),
        "w0_f2": f("L0_ffn_w2"), "b0_f2": f("L0_ffn_b2").reshape(1, -1),
        "w1_inp": np.ascontiguousarray(f("L1_attn_in_proj_weight").T),
        "b1_inp": f("L1_attn_in_proj_bias").reshape(1, -1),
        "w1_outp": np.ascontiguousarray(f("L1_attn_out_proj_weight").T),
        "b1_outp": f("L1_attn_out_proj_bias").reshape(1, -1),
        "w1_gate": f("L1_ffn_w_gate"), "w1_up": f("L1_ffn_w_up"),
        "w1_down": f("L1_ffn_w_down"),
        "w2_qkv": np.ascontiguousarray(np.concatenate(
            [f("L2_attn_Wq").T, f("L2_attn_Wk").T, f("L2_attn_Wv").T], axis=1)),
        "b2_qkv": np.concatenate(
            [f("L2_attn_bq"), f("L2_attn_bk"), f("L2_attn_bv")]).reshape(1, -1),
        "w2_o": np.ascontiguousarray(f("L2_attn_Wo").T),
        "b2_o": f("L2_attn_bo").reshape(1, -1),
        "w2_pk": np.ascontiguousarray(
            np.asarray(inputs["L2_ffn_packed_weight"], np.int32).reshape(256, 1024)),
        "w2_scale": f("L2_ffn_weight_scale").reshape(1, 1),
        "b2_f": f("L2_ffn_bias").reshape(1, -1),
        "w3_qkv": np.ascontiguousarray(np.concatenate(
            [f("L3_attn_Wq"), f("L3_attn_Wk"), f("L3_attn_Wv")], axis=1)),
        "b3_qkv": np.concatenate(
            [f("L3_attn_bq"), f("L3_attn_bk"), f("L3_attn_bv")]).reshape(1, -1),
        "w3_l": f("L3_attn_Wl"), "b3_l": f("L3_attn_bl").reshape(1, -1),
        "w3_o": f("L3_attn_Wo"), "b3_o": f("L3_attn_bo").reshape(1, -1),
        "w3_gate": f("L3_ffn_w_gate"), "w3_up": f("L3_ffn_w_up"),
        "w3_down": f("L3_ffn_w_down"),
        "ident": np.eye(P, dtype=np.float32),
        "costab": costab, "sintab": sintab, "masks": masks,
    }
    in_maps = []
    for c in range(NCORE):
        m = dict(common)
        m["x_in"] = np.ascontiguousarray(x[2 * c:2 * c + 2].reshape(M, D))
        in_maps.append(m)
    import os
    _dbg = os.environ.get("KDBG") or None
    if _dbg:
        prog = build(_dbg)
    else:
        if _NC is None:
            _NC = build()
        prog = _NC
    res = run_bass_kernel_spmd(prog, in_maps, core_ids=list(range(NCORE)))
    outs = [np.asarray(res.results[c]["out"], np.float32) for c in range(NCORE)]
    return np.concatenate(outs, axis=0).reshape(16, 2048, 1024)
